# version 11
# speedup vs baseline: 1.6672x; 1.0128x over previous
import math
import numpy as np
import ml_dtypes
from contextlib import ExitStack

import concourse.bass as bass
import concourse.mybir as mybir
from concourse.bass_utils import run_bass_kernel_spmd
from concourse.bass_core_rust import IndirectOffsetOnAxis

F32 = mybir.dt.float32
BF16 = mybir.dt.bfloat16
I32 = mybir.dt.int32
F16 = mybir.dt.float16
AF = mybir.ActivationFunctionType
ALU = mybir.AluOpType
AX = mybir.AxisListType

B, S, D = 2, 4096, 2048
HFF = 8192
NH, DH = 16, 128
IFF = 5632
NCH = 8
CH = S // NCH
TLOC = 2 * CH
NTT = TLOC // 128
NDT = D // 128
DENOM = math.sqrt(D) / S
LN_EPS = 1e-5
RMS_EPS = 1e-6
ISQD = 1.0 / math.sqrt(DH)
BSCALE = -1.0 / (2.0 * math.sqrt(DH))
FWHT_SCALE = 1.0 / math.sqrt(HFF)

RBLK = 3 * D + 2 * NH
NIT = IFF // 128
NGU = 2 * NIT

PHASES = ("A", "B", "C", "D")
DEV_TAPS = []
DEV_DEQ_ONLY = False
DEV_D_STOP = "full"


class Sem:
    __slots__ = ("h", "n")

    def __init__(self, ctx, nc, name):
        self.h = ctx.enter_context(nc.semaphore(name))
        self.n = 0

    def inc(self, ins, k=1):
        ins.then_inc(self.h, k)
        self.n += k
        return ins


class Ctx:
    pass


def on(c, eng_name, f):
    getattr(c.block, eng_name)(f)


def build_program(phases=PHASES):
    nc = bass.Bass("TRN2", target_bir_lowering=False)
    c = Ctx()
    c.nc = nc
    c.phases = phases

    c.x_in = nc.dram_tensor("x_in", [TLOC, D], F32, kind="ExternalInput").ap()
    c.y_out = nc.dram_tensor("y_out", [TLOC, D], F32, kind="ExternalOutput").ap()

    if "A" in phases:
        c.wq_t = nc.dram_tensor("wq_t", [D, D], F32, kind="ExternalInput").ap()
        c.wk_t = nc.dram_tensor("wk_t", [D, D], F32, kind="ExternalInput").ap()
        c.wv_t = nc.dram_tensor("wv_t", [D, D], F32, kind="ExternalInput").ap()
        c.wo_t = nc.dram_tensor("wo_t", [D, D], F32, kind="ExternalInput").ap()
        c.carry_sel = nc.dram_tensor("carry_sel", [16, 2], F32, kind="ExternalInput").ap()
        c.ag1_in = nc.dram_tensor("ag1_in", [2, D], F32).ap()
        c.ag1_out = nc.dram_tensor("ag1_out", [16, D], F32).ap()
    if "B" in phases:
        c.w1_t = nc.dram_tensor("w1_t", [D, HFF], F32, kind="ExternalInput").ap()
        c.w3_t = nc.dram_tensor("w3_t", [D, HFF], F32, kind="ExternalInput").ap()
        c.w2_t = nc.dram_tensor("w2_t", [HFF, D], F32, kind="ExternalInput").ap()
        c.w2a = nc.dram_tensor("w2a", [HFF, D], BF16).ap()
        c.w2b = nc.dram_tensor("w2b", [HFF, D], BF16).ap()
    if "C" in phases:
        c.w_in = nc.dram_tensor("w_in", [D, 3 * D], F32, kind="ExternalInput").ap()
        c.b_in = nc.dram_tensor("b_in", [1, 3 * D], F32, kind="ExternalInput").ap()
        c.w_out = nc.dram_tensor("w_out", [D, D], F32, kind="ExternalInput").ap()
        c.b_out = nc.dram_tensor("b_out", [1, D], F32, kind="ExternalInput").ap()
        c.gidx = nc.dram_tensor("gidx", [128, 128], I32, kind="ExternalInput").ap()
        c.regt = nc.dram_tensor("regt", [1, 64], I32, kind="ExternalInput").ap()
        c.agc_in = nc.dram_tensor("agc_in", [RBLK, TLOC], BF16).ap()
        c.agc_out_h = nc.dram_tensor("agc_out", [4 * RBLK, TLOC], BF16)
        c.agc_out = c.agc_out_h.ap()
        c.ag2_in = nc.dram_tensor("ag2_in", [512, S], BF16).ap()
        c.ag2_out_h = nc.dram_tensor("ag2_out", [4 * 512, S], BF16)
        c.ag2_out = c.ag2_out_h.ap()
        c.rb = nc.dram_tensor("rb", [32, 128], F32).ap()
    if "D" in phases:
        c.gup_sl = nc.dram_tensor("gup_sl", [11 * 128, 256], I32, kind="ExternalInput").ap()
        c.gus_sl = nc.dram_tensor("gus_sl", [11 * 128, 16], F16, kind="ExternalInput").ap()
        c.dnp_sl = nc.dram_tensor("dnp_sl", [6 * 128, 256], I32, kind="ExternalInput").ap()
        c.dns_sl = nc.dram_tensor("dns_sl", [6 * 128, 16], F16, kind="ExternalInput").ap()
        c.dq_in = nc.dram_tensor("dq_in", [17 * D, 128], BF16).ap()
        c.dq_out_h = nc.dram_tensor("dq_out", [8 * 17 * D, 128], BF16)
        c.dq_out = c.dq_out_h.ap()
        c.wdn_t = nc.dram_tensor("wdn_t", [IFF, D], BF16).ap()

    c.cst_f = nc.dram_tensor("cst_f", [128, 256], F32, kind="ExternalInput").ap()
    c.cst2_f = nc.dram_tensor("cst2_f", [64, 64], F32, kind="ExternalInput").ap()

    with ExitStack() as ctx:
        c.ctx = ctx

        def sb(name, shape, dt):
            return ctx.enter_context(nc.sbuf_tensor(name, shape, dt))

        c.AR = [sb(f"AR{i}", [128, 16384], BF16) for i in range(5)]
        c.AR.append(sb("AR5", [128, 8192], BF16))
        c.SM = sb("SM", [128, 10752], BF16)
        c.ident = sb("ident", [128, 128], BF16)
        c.identf = sb("identf", [128, 128], F32)
        c.tri = sb("tri", [128, 128], F32)
        c.h128 = sb("h128", [128, 128], BF16)
        c.h64 = sb("h64", [64, 64], BF16)
        c.ones_row = sb("ones_row", [1, 512], BF16)
        c.ones_t = sb("ones_t", [128, 512], F32)
        c.stat = sb("stat", [128, 64], F32)
        c.stat2 = sb("stat2", [128, 64], F32)
        c.iob = sb("iob", [128, 96], I32)
        c.selS = sb("selS", [16, 2], F32)
        if "C" in phases:
            c.bias_ch = sb("bias_ch", [1, 1024], BF16)

        smf = c.SM.ap().bitcast(F32)
        smb = c.SM.ap()
        smi = c.SM.ap().bitcast(I32)
        def fv(off_b, n, r0=0, nr=128):
            return smf[r0:r0 + nr, off_b // 4: off_b // 4 + n]
        def bv(off_b, n, r0=0, nr=128):
            return smb[r0:r0 + nr, off_b // 2: off_b // 2 + n]
        def iv(off_b, n, r0=0, nr=128):
            return smi[r0:r0 + nr, off_b // 4: off_b // 4 + n]
        c.ssum = fv(0, 32)
        c.carT = fv(128, 32)
        c.rtmp = fv(256, 128, 0, 64)
        c.agS = fv(768, 2048, 0, 16)
        c.junkA = bv(8960, 2048)
        c.junkB = bv(0, 2048)
        c.sgB = bv(4096, 512)
        c.pb = bv(0, 2048)
        c.ptb = bv(4096, 2048)
        c.qn2 = fv(8192, 128)
        c.kn2 = fv(8704, 128)
        c.knT = bv(9216, 1024, 0, 16)
        c.qnT = bv(11264, 1024, 0, 16)
        c.biasH = fv(13312, 32)
        c.rsbC = fv(13440, 32)
        c.rsumC = fv(13568, 4)
        c.qnTh = fv(13632, 32)
        c.recipT = fv(13760, 128, 0, 4)
        c.rrow = bv(14272, 1024, 0, 1)
        c.rsbH = fv(17472, 256)
        c.rsumH = fv(18496, 32)
        c.recipH = fv(18688, 128, 0, 32)
        c.tmpPV = fv(15360, 512)
        c.kmaxv = bv(17408, 2, 0, 1)
        c.junkC = bv(0, 2048)
        if "C" in phases:
            c.regt_sb = sb("regt_sb", [1, 64], I32)
            c.gidx_sb = sb("gidx_sb", [128, 128], I32)
        c.pk_sb = iv(0, 704)
        c.sc_sb = fv(2816, 44)
        c.stgD = fv(4096, 2048)
        c.junkD = bv(12288, 2048)

        c.PS = [ctx.enter_context(nc.psum_tensor(f"PS{i}", [128, 512], F32))
                for i in range(6)]
        c.PT = [ctx.enter_context(nc.psum_tensor(f"PT{i}", [128, 1024], BF16))
                for i in range(2)]

        for nm in ["dX", "dW", "dS", "dO", "mm", "ev", "sc", "gp", "cc",
                   "dwA", "dwB", "dsA", "dsB"]:
            setattr(c, nm, Sem(ctx, nc, nm))
        c.dwS = [c.dwA, c.dwB]
        c.dsS = [c.dsA, c.dsB]

        c.block = ctx.enter_context(nc.Block())
        _emit(c)

    return nc


def _tap(c, name, src_ap, rows=128):
    if name not in DEV_TAPS:
        return
    dt_out = nc_dt = F32
    t = c.nc.dram_tensor("tap_" + name, [rows, src_ap.shape[-1] if hasattr(src_ap, 'shape') else 0], dt_out, kind="ExternalOutput").ap()

    def g(g):
        g.wait_ge(c.ev.h, c.ev.n)
        g.wait_ge(c.sc.h, c.sc.n)
        g.wait_ge(c.mm.h, c.mm.n)
        ins = g.dma_start(t, src_ap)
        c.dO.inc(ins, 16)
    on(c, "gpsimd", g)

    def vw(v):
        v.wait_ge(c.dO.h, c.dO.n)
    on(c, "vector", vw)

    def tw(t):
        t.wait_ge(c.dO.h, c.dO.n)
    on(c, "tensor", tw)


def _emit(c):
    _setup(c)
    if "A" in c.phases:
        _phase_a(c)
    if "B" in c.phases:
        _w2_transform(c)
        _phase_b(c)
    if "C" in c.phases:
        _phase_c(c)
    if "D" in c.phases:
        _dequant(c)
        if not DEV_DEQ_ONLY:
            _phase_d(c)
    if c.phases[-1] != "D" or DEV_DEQ_ONLY:
        _store_stream_dev(c)
    _final_wait(c)


def _setup(c):
    def g(g):
        ins = g.dma_start(c.AR[0].ap().rearrange("p (a d) -> p a d", a=NTT),
                          c.x_in.rearrange("(a p) d -> p a d", p=128))
        c.dX.inc(ins, 16)
        ins = g.dma_start(c.tri[:, :], c.cst_f[:, 0:128])
        c.dX.inc(ins, 16)
        ins = g.dma_start(c.h128[:, :], c.cst_f[:, 128:256])
        c.dX.inc(ins, 16)
        ins = g.dma_start(c.h64[:, :], c.cst2_f[:, :])
        c.dX.inc(ins, 16)
        if "A" in c.phases:
            ins = g.dma_start(c.selS[:, :], c.carry_sel)
            c.dX.inc(ins, 16)
        if "C" in c.phases:
            ins = g.dma_start(c.gidx_sb[:, :], c.gidx)
            c.dX.inc(ins, 16)
            ins = g.dma_start(c.regt_sb[:, :], c.regt)
            c.dX.inc(ins, 16)
        g.memset(c.ones_row[:, :], 1.0)
        g.memset(c.ones_t[:, :], 1.0)
        g.iota(c.iob[:, 64:65], [[0, 1]], channel_multiplier=1,
               allow_small_or_imprecise_dtypes=True)
        ins = g.iota(c.iob[:, 0:32], [[1, 32]], channel_multiplier=0,
                     allow_small_or_imprecise_dtypes=True)
        c.gp.inc(ins)
    on(c, "gpsimd", g)

    def v(v):
        v.wait_ge(c.gp.h, c.gp.n)
        for q in range(4):
            v.tensor_scalar(c.iob[:, 32:64], c.iob[:, 0:32], q * 32, None, ALU.add)
            v.drain()
            v.tensor_tensor(c.ident[:, q * 32:(q + 1) * 32], c.iob[:, 32:64],
                            c.iob[:, 64:65].broadcast_to([128, 32]), ALU.is_equal)
            v.drain()
        ins = v.tensor_copy(c.identf[:, :], c.ident[:, :])
        c.ev.inc(ins)
    on(c, "vector", v)


def _store_stream_dev(c):
    def g(g):
        g.wait_ge(c.ev.h, c.ev.n)
        g.wait_ge(c.sc.h, c.sc.n)
        ins = g.dma_start(c.y_out.rearrange("(a p) d -> p a d", p=128),
                          c.AR[0].ap().rearrange("p (a d) -> p a d", a=NTT))
        c.dO.inc(ins, 16)
    on(c, "gpsimd", g)


def _final_wait(c):
    def g(g):
        g.wait_ge(c.dO.h, c.dO.n)
    on(c, "gpsimd", g)



def pipe_loop(c, n, load_fn, fill_fn, evac_fn, pre_gate=None):
    fill_mark = [0] * (n + 4)
    evac_mark = [0] * (n + 4)
    load_mark = [0] * (n + 4)

    def emit_load(i):
        def g(g, i=i):
            if i >= 2:
                t_mark = fill_mark[i - 2]
                g.wait_ge(c.mm.h, t_mark)
            load_fn(g, i)
        on(c, "gpsimd", g)
        load_mark[i] = c.dW.n

    emit_load(0)
    if n > 1:
        emit_load(1)
    for i in range(n):
        bank = c.PS[i % 2]

        def t(t, i=i, bank=bank):
            t.wait_ge(c.dW.h, load_mark[i])
            if i >= 2:
                t.wait_ge(c.ev.h, evac_mark[i - 2])
            if pre_gate is not None and i == 0:
                pre_gate(t)
            fill_fn(t, i, bank)
        on(c, "tensor", t)
        fill_mark[i] = c.mm.n

        def v(v, i=i, bank=bank):
            v.wait_ge(c.mm.h, fill_mark[i])
            evac_fn(v, i, bank)
        on(c, "vector", v)
        evac_mark[i] = c.ev.n

        if i + 2 < n:
            emit_load(i + 2)


def transpose_into(c, pairs):
    n = len(pairs)
    groups = [(i, min(i + 8, n)) for i in range(0, n, 8)]
    for gi, (lo, hi) in enumerate(groups):
        bank = c.PT[gi % 2]

        def t(t, lo=lo, hi=hi, bank=bank):
            t.wait_ge(c.ev.h, c.ev.n)
            ins = None
            for i in range(lo, hi):
                ins = t.transpose(bank[:, (i - lo) * 128:(i - lo + 1) * 128],
                                  pairs[i][0], c.ident[:, :])
            c.mm.inc(ins)
        on(c, "tensor", t)

        def v(v, lo=lo, hi=hi, bank=bank):
            v.wait_ge(c.mm.h, c.mm.n)
            ins = None
            for i in range(lo, hi):
                ins = v.tensor_copy(pairs[i][1], bank[:, (i - lo) * 128:(i - lo + 1) * 128])
            c.ev.inc(ins)
        on(c, "vector", v)


def transpose_stream(c, src_arena, dst_arena):
    sa = src_arena.ap().rearrange("p (a d) -> p a d", a=NTT)
    da = dst_arena.ap().rearrange("p (k t) -> p k t", k=NDT)
    pairs = []
    for dt in range(NDT):
        for tt in range(NTT):
            pairs.append((sa[:, tt, dt * 128:(dt + 1) * 128],
                          da[:, dt, tt * 128:(tt + 1) * 128]))
    transpose_into(c, pairs)


def stream_proj(c, w_dram, dst_arena, w_cols=D, pre_gate=None):
    xt = c.AR[1].ap().rearrange("p (k t) -> p k t", k=NDT)
    wbh = [c.AR[5].ap()[:, 0:4096].rearrange("p (k e) -> p k e", k=NDT),
           c.AR[5].ap()[:, 4096:8192].rearrange("p (k e) -> p k e", k=NDT)]
    da = dst_arena.ap().rearrange("p (k t) -> p k t", k=NDT)
    nit = w_cols // 256

    load_mark, fill_mark, evac_mark = {}, {}, {}
    ev_base = c.ev.n
    mm_base = c.mm.n

    def emit_load(i):
        sem = c.dwS[i % 2]

        def g(g, i=i, sem=sem):
            g.wait_ge(c.mm.h, fill_mark.get(i - 2, mm_base))
            ins = g.dma_start(
                wbh[i % 2][:, :, 0:256],
                w_dram.rearrange("(k p) e -> p k e", p=128)[:, :, i * 256:(i + 1) * 256])
            sem.inc(ins, 16)
        on(c, "gpsimd", g)
        load_mark[i] = c.dwS[i % 2].n

    emit_load(0)
    if nit > 1:
        emit_load(1)
    si = 0
    for i in range(nit):
        for ei in range(2):
            for tc in range(2):
                bank = c.PS[si % 3]

                def t(t, si=si, i=i, ei=ei, tc=tc, bank=bank):
                    t.wait_ge(c.dwS[i % 2].h, load_mark[i])
                    t.wait_ge(c.ev.h, evac_mark.get(si - 3, ev_base))
                    if pre_gate is not None and si == 0:
                        pre_gate(t)
                    wb = wbh[i % 2]
                    ins = None
                    for kt in range(NDT):
                        ins = t.matmul(bank[:, :], wb[:, kt, ei * 128:(ei + 1) * 128],
                                       xt[:, kt, tc * 512:(tc + 1) * 512],
                                       start=(kt == 0), stop=(kt == NDT - 1))
                    c.mm.inc(ins)
                on(c, "tensor", t)
                fill_mark[i] = c.mm.n

                def v(v, si=si, i=i, ei=ei, tc=tc, bank=bank):
                    v.wait_ge(c.mm.h, fill_mark[i])
                    ins = v.tensor_copy(da[:, i * 2 + ei, tc * 512:(tc + 1) * 512],
                                        bank[:, :])
                    c.ev.inc(ins)
                on(c, "vector", v)
                evac_mark[si] = c.ev.n
                si += 1
        if i + 2 < nit:
            emit_load(i + 2)


def out_proj(c, stat_arena, w_dram, x_prev, vpre, scale, bias_dram,
             wbuf2=None, w_cols=D):
    sa = stat_arena.ap().rearrange("p (k t) -> p k t", k=NDT)
    bufs = [c.AR[5].ap()[:, 0:8192].rearrange("p (k e) -> p k e", k=NDT)]
    if wbuf2 is not None:
        bufs.append(wbuf2.rearrange("p (k e) -> p k e", k=NDT))
    xa = x_prev.ap().rearrange("p (a d) -> p a d", a=NTT)
    va = vpre.ap().rearrange("p (a d) -> p a d", a=NTT)
    nfc = w_cols // 512
    nb = len(bufs)
    load_mark, fill_mark, evac_mark = {}, {}, {}
    ev_base = c.ev.n
    mm_base = c.mm.n

    def emit_load(i):
        sem = c.dwS[i % 2]

        def g(g, i=i, sem=sem):
            g.wait_ge(c.mm.h, fill_mark.get(i - nb, mm_base))
            if i < nb:
                g.wait_ge(c.ev.h, ev_base)
            ins = g.dma_start(
                bufs[i % nb][:, :, 0:512],
                w_dram.rearrange("(k p) e -> p k e", p=128)[:, :, i * 512:(i + 1) * 512])
            sem.inc(ins, 16)
            if bias_dram is not None:
                ins = g.dma_start(c.bias_ch[0:1, (i % 2) * 512:(i % 2) * 512 + 512],
                                  bias_dram[0:1, i * 512:(i + 1) * 512])
                sem.inc(ins, 16)
        on(c, "gpsimd", g)
        load_mark[i] = c.dwS[i % 2].n

    for i in range(min(nb, nfc)):
        emit_load(i)

    si = 0
    for fc in range(nfc):
        for tt in range(NTT):
            bank = c.PS[si % 3]

            def t(t, fc=fc, tt=tt, bank=bank, si=si):
                t.wait_ge(c.dwS[fc % 2].h, load_mark[fc])
                t.wait_ge(c.ev.h, evac_mark.get(si - 3, ev_base))
                wb = bufs[fc % nb]
                ins = None
                first = True
                if bias_dram is not None:
                    ins = t.matmul(bank[:, :], c.ones_row[0:1, 0:128],
                                   c.bias_ch[0:1, (fc % 2) * 512:(fc % 2) * 512 + 512],
                                   start=True, stop=False)
                    first = False
                for et in range(NDT):
                    ins = t.matmul(bank[:, :], sa[:, et, tt * 128:(tt + 1) * 128],
                                   wb[:, et, 0:512],
                                   start=first, stop=(et == NDT - 1))
                    first = False
                c.mm.inc(ins)
            on(c, "tensor", t)
            fill_mark[fc] = c.mm.n

            def v(v, fc=fc, tt=tt, bank=bank, si=si):
                v.wait_ge(c.mm.h, fill_mark[fc])
                ins = v.scalar_tensor_tensor(
                    va[:, tt, fc * 512:(fc + 1) * 512], bank[:, :], scale,
                    xa[:, tt, fc * 512:(fc + 1) * 512],
                    ALU.mult, ALU.add,
                    accum_out=c.stat[:, tt * 4 + fc:tt * 4 + fc + 1])
                c.ev.inc(ins)
            on(c, "vector", v)
            evac_mark[si] = c.ev.n
            si += 1
        if fc + nb < nfc:
            emit_load(fc + nb)


def ln_epilogue(c, v_arena, out_arena, eps, junk=None):
    va = v_arena.ap().rearrange("p (a d) -> p a d", a=NTT)
    oa = out_arena.ap().rearrange("p (a d) -> p a d", a=NTT)

    def v1(v):
        v.wait_ge(c.ev.h, c.ev.n)
        v.drain()
        v.tensor_reduce(c.stat[:, 32:40],
                        c.stat.ap().rearrange("p (a f) -> p a f", a=16)[:, 0:8, 0:4],
                        AX.X, ALU.add)
        v.drain()
        v.tensor_scalar(c.stat[:, 32:40], c.stat[:, 32:40], 1.0 / D, None, ALU.mult)
        v.drain()
        ins = v.tensor_scalar(c.stat[:, 40:48], c.stat[:, 32:40], -1.0, None, ALU.mult)
        c.ev.inc(ins)
    on(c, "vector", v1)

    def s1(s):
        s.wait_ge(c.ev.h, c.ev.n)
        ins = None
        for tt in range(NTT):
            ins = s.activation(junk, va[:, tt, :],
                               AF.Square, bias=c.stat[:, 32 + tt:33 + tt], scale=-1.0,
                               accum_out=c.stat2[:, tt:tt + 1])
        c.sc.inc(ins)
    on(c, "scalar", s1)

    def v2(v):
        v.wait_ge(c.sc.h, c.sc.n)
        ins = v.tensor_scalar(c.stat[:, 48:56], c.stat2[:, 0:8], 1.0 / D, eps,
                              ALU.mult, ALU.add)
        c.ev.inc(ins)
    on(c, "vector", v2)

    def s2(s):
        s.wait_ge(c.ev.h, c.ev.n)
        s.activation(c.stat[:, 48:56], c.stat[:, 48:56], AF.Ln)
        s.drain()
        ins = s.activation(c.stat[:, 48:56], c.stat[:, 48:56], AF.Exp, scale=-0.5)
        c.sc.inc(ins)
    on(c, "scalar", s2)

    def v3(v):
        v.wait_ge(c.sc.h, c.sc.n)
        ins = None
        for tt in range(NTT):
            ins = v.tensor_scalar(oa[:, tt, :], va[:, tt, :],
                                  c.stat[:, 32 + tt:33 + tt],
                                  c.stat[:, 48 + tt:49 + tt],
                                  ALU.subtract, ALU.mult)
        c.ev.inc(ins)
    on(c, "vector", v3)


def ln_epilogue_f32(c, v_arena, eps, junk):
    va = v_arena.ap().rearrange("p (a d) -> p a d", a=NTT)

    def v1(v):
        v.wait_ge(c.ev.h, c.ev.n)
        v.drain()
        v.tensor_reduce(c.stat[:, 32:40],
                        c.stat.ap().rearrange("p (a f) -> p a f", a=16)[:, 0:8, 0:4],
                        AX.X, ALU.add)
        v.drain()
        v.tensor_scalar(c.stat[:, 32:40], c.stat[:, 32:40], 1.0 / D, None, ALU.mult)
        v.drain()
        ins = v.tensor_scalar(c.stat[:, 40:48], c.stat[:, 32:40], -1.0, None, ALU.mult)
        c.ev.inc(ins)
    on(c, "vector", v1)

    def s1(s):
        s.wait_ge(c.ev.h, c.ev.n)
        ins = None
        for tt in range(NTT):
            ins = s.activation(junk, va[:, tt, :],
                               AF.Square, bias=c.stat[:, 32 + tt:33 + tt], scale=-1.0,
                               accum_out=c.stat2[:, tt:tt + 1])
        c.sc.inc(ins)
    on(c, "scalar", s1)

    def v2(v):
        v.wait_ge(c.sc.h, c.sc.n)
        ins = v.tensor_scalar(c.stat[:, 48:56], c.stat2[:, 0:8], 1.0 / D, eps,
                              ALU.mult, ALU.add)
        c.ev.inc(ins)
    on(c, "vector", v2)

    def s2(s):
        s.wait_ge(c.ev.h, c.ev.n)
        s.activation(c.stat[:, 48:56], c.stat[:, 48:56], AF.Ln)
        s.drain()
        ins = s.activation(c.stat[:, 48:56], c.stat[:, 48:56], AF.Exp, scale=-0.5)
        c.sc.inc(ins)
    on(c, "scalar", s2)

    for tt in range(NTT):
        def v3(v, tt=tt):
            v.wait_ge(c.sc.h, c.sc.n)
            v.wait_ge(c.dO.h, c.dO.n)
            ins = v.tensor_scalar(c.stgD, va[:, tt, :],
                                  c.stat[:, 32 + tt:33 + tt],
                                  c.stat[:, 48 + tt:49 + tt],
                                  ALU.subtract, ALU.mult)
            c.ev.inc(ins)
        on(c, "vector", v3)

        def sy(sy, tt=tt):
            sy.wait_ge(c.ev.h, c.ev.n)
            ins = sy.dma_start(c.y_out[tt * 128:(tt + 1) * 128, :], c.stgD)
            c.dO.inc(ins, 16)
        on(c, "sync", sy)



def _phase_a(c):
    AR = c.AR
    XT, KT, VQ, UB = AR[1], AR[2], AR[3], AR[4]

    def t0(t):
        t.wait_ge(c.dX.h, c.dX.n)
        t.wait_ge(c.ev.h, c.ev.n)
    on(c, "tensor", t0)

    transpose_stream(c, AR[0], XT)

    stream_proj(c, c.wk_t, KT)
    stream_proj(c, c.wv_t, VQ)

    kt_a = KT.ap().rearrange("p (k t) -> p k t", k=NDT)
    vq_a = VQ.ap().rearrange("p (k t) -> p k t", k=NDT)
    ub_a = UB.ap().rearrange("p (k t) -> p k t", k=NDT)

    def vu(v):
        v.wait_ge(c.ev.h, c.ev.n)
        for tc in range(2):
            for et in range(NDT):
                v.tensor_tensor(ub_a[:, et, tc * 512:(tc + 1) * 512],
                                kt_a[:, et, tc * 512:(tc + 1) * 512],
                                vq_a[:, et, tc * 512:(tc + 1) * 512], ALU.mult)
        v.drain()
        ins = None
        for tc in range(2):
            for et in range(NDT):
                ins = v.tensor_reduce(c.ssum[:, tc * 16 + et:tc * 16 + et + 1],
                                      ub_a[:, et, tc * 512:(tc + 1) * 512],
                                      AX.X, ALU.add)
        c.ev.inc(ins)
    on(c, "vector", vu)

    stream_proj(c, c.wq_t, VQ)

    def tss(t):
        t.wait_ge(c.ev.h, c.ev.n)
        ins = t.transpose(c.PS[2].ap()[0:32, 0:128], c.ssum,
                          c.identf[:, :])
        c.mm.inc(ins)
    on(c, "tensor", tss)

    def vss(v):
        v.wait_ge(c.mm.h, c.mm.n)
        ins = v.tensor_copy(c.rtmp[0:32, :], c.PS[2].ap()[0:32, 0:128])
        c.ev.inc(ins)
    on(c, "vector", vss)

    def gss(g):
        g.wait_ge(c.ev.h, c.ev.n)
        for tc in range(2):
            ins = g.dma_start(
                c.ag1_in[tc:tc + 1, :].rearrange("o (e p) -> (o e) p", p=128),
                c.rtmp[tc * 16:(tc + 1) * 16, :])
            c.dS.inc(ins, 16)
        g.wait_ge(c.dS.h, c.dS.n)
        ins = g.collective_compute(
            "AllGather", ALU.bypass, replica_groups=[list(range(8))],
            ins=[c.ag1_in], outs=[c.ag1_out])
        c.cc.inc(ins)
        g.wait_ge(c.cc.h, c.cc.n)
        ins = g.dma_start(c.agS, c.ag1_out)
        c.dS.inc(ins, 16)
    on(c, "gpsimd", gss)

    def tcar(t):
        t.wait_ge(c.dS.h, c.dS.n)
        ins = None
        for et in range(NDT):
            ins = t.matmul(c.PS[3].ap()[0:128, et * 2:(et + 1) * 2],
                           c.agS[:, et * 128:(et + 1) * 128],
                           c.selS[0:16, 0:2], start=True, stop=True)
        c.mm.inc(ins)
    on(c, "tensor", tcar)

    def vcar(v):
        v.wait_ge(c.mm.h, c.mm.n)
        ins = v.tensor_copy(c.carT, c.PS[3].ap()[0:128, 0:32])
        c.ev.inc(ins)
    on(c, "vector", vcar)

    def vscan(v):
        v.drain()
        ins = None
        for et in range(NDT):
            for tc in range(2):
                ins = v.tensor_tensor_scan(
                    kt_a[:, et, tc * 512:(tc + 1) * 512],
                    c.ones_t[:, 0:512],
                    ub_a[:, et, tc * 512:(tc + 1) * 512],
                    c.carT[:, et * 2 + tc:et * 2 + tc + 1],
                    ALU.mult, ALU.add)
                v.drain()
        c.ev.inc(ins)
    on(c, "vector", vscan)

    _tap(c, "u", c.AR[4].ap())
    _tap(c, "kvt", c.AR[2].ap())
    _tap(c, "car", c.carT)
    _tap(c, "q", c.AR[3].ap())

    def vvh(v):
        v.drain()
        ins = None
        for et in range(NDT):
            ins = v.tensor_tensor(vq_a[:, et, :], vq_a[:, et, :], kt_a[:, et, :],
                                  ALU.mult)
        c.ev.inc(ins)
    on(c, "vector", vvh)

    _tap(c, "vhat", c.AR[3].ap())
    out_proj(c, VQ, c.wo_t, AR[0], UB, DENOM, None, wbuf2=KT.ap()[:, 0:8192])
    _tap(c, "vpre", c.AR[4].ap())
    _tap(c, "stat", c.stat[:, :])
    ln_epilogue(c, UB, AR[0], LN_EPS, junk=c.junkA)



def _w2_transform(c):
    wt = c.AR[5].ap()
    for stage in range(2):
        ntile = 64 if stage == 0 else 128
        npart = 128 if stage == 0 else 64
        stat_c = c.h128 if stage == 0 else c.h64
        scl = 1.0 if stage == 0 else FWHT_SCALE
        ldm, flm, evm, stm = {}, {}, {}, {}
        mm_base, ev_base = c.mm.n, c.ev.n
        st_base = [c.dsS[0].n, c.dsS[1].n]
        stage1_gate = [c.dsS[0].n, c.dsS[1].n]

        def emit_load(T, stage=stage):
            sem = c.dwS[T % 2]

            def g(g, T=T, stage=stage, sem=sem):
                g.wait_ge(c.mm.h, flm.get(T - 2, mm_base))
                if stage == 1 and T < 2:
                    g.wait_ge(c.dsS[0].h, stage1_gate[0])
                    g.wait_ge(c.dsS[1].h, stage1_gate[1])
                sl = (T % 2) * 2048
                if stage == 0:
                    ins = g.dma_start(wt[:, sl:sl + 2048],
                                      c.w2_t[T * 128:(T + 1) * 128, :])
                else:
                    ins = g.dma_start(
                        wt[0:64, sl:sl + 2048],
                        c.w2a.rearrange("(t p) d -> p t d", p=128)[T, :, :])
                sem.inc(ins, 16)
            on(c, "gpsimd", g)
            ldm[T] = c.dwS[T % 2].n

        emit_load(0)
        emit_load(1)
        evs = {}
        ss = 0
        for T in range(ntile):
            sl = (T % 2) * 2048
            so = 4096 + (T % 2) * 2048
            for jh in range(2):
                b0 = (ss % 3) * 2

                def t(t, T=T, jh=jh, b0=b0, ss=ss, stage=stage, npart=npart,
                      stat_c=stat_c, sl=sl):
                    if jh == 0:
                        t.wait_ge(c.dwS[T % 2].h, ldm[T])
                    t.wait_ge(c.ev.h, evs.get(ss - 3, ev_base))
                    ins = None
                    for j in (jh * 2, jh * 2 + 1):
                        ins = t.matmul(c.PS[b0 + j % 2].ap()[0:npart, :],
                                       stat_c[0:npart, 0:npart],
                                       wt[0:npart, sl + j * 512:sl + (j + 1) * 512],
                                       start=True, stop=True)
                    c.mm.inc(ins)
                on(c, "tensor", t)
                flm[(T, jh)] = c.mm.n

                def v(v, T=T, jh=jh, b0=b0, ss=ss, stage=stage, scl=scl,
                      npart=npart, so=so):
                    v.wait_ge(c.mm.h, flm[(T, jh)])
                    if jh == 0:
                        v.wait_ge(c.dsS[T % 2].h, stm.get(T - 2, st_base[T % 2]))
                    ins = None
                    for j in (jh * 2, jh * 2 + 1):
                        ins = v.tensor_scalar(wt[0:npart, so + j * 512:so + (j + 1) * 512],
                                              c.PS[b0 + j % 2].ap()[0:npart, :], scl, None,
                                              ALU.mult)
                    c.ev.inc(ins)
                on(c, "vector", v)
                evs[ss] = c.ev.n
                ss += 1
            flm[T] = c.mm.n
            evm[T] = c.ev.n

            sem2 = c.dsS[T % 2]

            def g2(g, T=T, stage=stage, npart=npart, so=so, sem2=sem2):
                g.wait_ge(c.ev.h, evm[T])
                if stage == 0:
                    ins = g.dma_start(c.w2a[T * 128:(T + 1) * 128, :],
                                      wt[:, so:so + 2048])
                else:
                    ins = g.dma_start(
                        c.w2b.rearrange("(t p) d -> p t d", p=128)[T, :, :],
                        wt[0:64, so:so + 2048])
                sem2.inc(ins, 16)
            on(c, "gpsimd", g2)
            stm[T] = c.dsS[T % 2].n
            if T + 2 < ntile:
                emit_load(T + 2)
def _phase_b(c):
    _tap(c, "w2a", c.w2a, rows=HFF)
    _tap(c, "w2b", c.w2b, rows=HFF)
    AR = c.AR
    XT, HB1, HB2, VB = AR[1], AR[2], AR[3], AR[4]
    xt = XT.ap().rearrange("p (k t) -> p k t", k=NDT)
    xa = AR[0].ap().rearrange("p (a d) -> p a d", a=NTT)
    va = VB.ap().rearrange("p (a d) -> p a d", a=NTT)
    wb5 = c.AR[5].ap()

    def tb0(t):
        t.wait_ge(c.ev.h, c.ev.n)
        t.wait_ge(c.sc.h, c.sc.n)
    on(c, "tensor", tb0)
    transpose_stream(c, AR[0], XT)

    NHT = HFF // 128

    def hb_slice(ht, a, b):
        if ht < 32:
            return HB1.ap()[:, ht * 512 + a: ht * 512 + b]
        return HB2.ap()[:, (ht - 32) * 512 + a: (ht - 32) * 512 + b]

    for tc in range(2):
        load_mark, fillA, fillB, scm, evm = {}, {}, {}, {}, {}
        mm_base, ev_base, sc_base = c.mm.n, c.ev.n, c.sc.n

        def emit_load(ht, tc=tc):
            sem = c.dwS[ht % 2]

            def g(g, ht=ht, sem=sem):
                g.wait_ge(c.mm.h, fillB.get(ht - 2, mm_base))
                sl = (ht % 2) * 4096
                ins = g.dma_start(
                    wb5[:, sl:sl + 2048].rearrange("p (k e) -> p k e", e=128),
                    c.w1_t.rearrange("(k p) e -> p k e", p=128)[:, :, ht * 128:(ht + 1) * 128])
                sem.inc(ins, 16)
                ins = g.dma_start(
                    wb5[:, sl + 2048:sl + 4096].rearrange("p (k e) -> p k e", e=128),
                    c.w3_t.rearrange("(k p) e -> p k e", p=128)[:, :, ht * 128:(ht + 1) * 128])
                sem.inc(ins, 16)
            on(c, "gpsimd", g)
            load_mark[ht] = c.dwS[ht % 2].n

        emit_load(0)
        emit_load(1)
        for ht in range(NHT):
            bp = (ht % 2) * 2
            sl = (ht % 2) * 4096

            def t(t, tc=tc, ht=ht, bp=bp, sl=sl):
                t.wait_ge(c.dwS[ht % 2].h, load_mark[ht])
                t.wait_ge(c.sc.h, scm.get(ht - 2, sc_base))
                t.wait_ge(c.ev.h, evm.get(ht - 2, ev_base))
                ins = None
                for kt in range(NDT):
                    ins = t.matmul(c.PS[bp][:, :], wb5[:, sl + kt * 128:sl + (kt + 1) * 128],
                                   xt[:, kt, tc * 512:(tc + 1) * 512],
                                   start=(kt == 0), stop=(kt == NDT - 1))
                c.mm.inc(ins)
                fillA[ht] = c.mm.n
                ins = None
                for kt in range(NDT):
                    ins = t.matmul(c.PS[bp + 1][:, :],
                                   wb5[:, sl + 2048 + kt * 128:sl + 2048 + (kt + 1) * 128],
                                   xt[:, kt, tc * 512:(tc + 1) * 512],
                                   start=(kt == 0), stop=(kt == NDT - 1))
                c.mm.inc(ins)
            on(c, "tensor", t)
            fillB[ht] = c.mm.n
            fillA[ht] = fillB[ht] - 1

            def sil(s, ht=ht, bp=bp):
                s.wait_ge(c.mm.h, fillA[ht])
                ins = s.activation(c.sgB, c.PS[bp][:, :], AF.Silu)
                c.sc.inc(ins)
            on(c, "scalar", sil)
            scm[ht] = c.sc.n

            def v(v, tc=tc, ht=ht, bp=bp):
                v.wait_ge(c.sc.h, scm[ht])
                v.wait_ge(c.mm.h, fillB[ht])
                ins = v.tensor_tensor(hb_slice(ht, 0, 512), c.sgB,
                                      c.PS[bp + 1][:, :], ALU.mult)
                c.ev.inc(ins)
            on(c, "vector", v)
            evm[ht] = c.ev.n
            if ht + 2 < NHT:
                emit_load(ht + 2)

        for dc in range(4):
            load_mark2, fill2 = {}, {}
            mm_base2, ev_base2, sc_base2 = c.mm.n, c.ev.n, c.sc.n

            def emit_load2(qg, dc=dc):
                sem = c.dwS[qg % 2]

                def g(g, qg=qg, dc=dc, sem=sem):
                    g.wait_ge(c.mm.h, fill2.get(qg - 2, mm_base2))
                    if qg < 2 and dc == 0:
                        g.wait_ge(c.dsS[0].h, c.dsS[0].n)
                        g.wait_ge(c.dsS[1].h, c.dsS[1].n)
                    sl = (qg % 2) * 4096
                    ins = g.dma_start(
                        wb5[:, sl:sl + 4096].rearrange("p (k e) -> p k e", e=512),
                        c.w2b.rearrange("(k p) e -> p k e", p=128)
                        [:, qg * 8:(qg + 1) * 8, dc * 512:(dc + 1) * 512])
                    sem.inc(ins, 16)
                on(c, "gpsimd", g)
                load_mark2[qg] = c.dwS[qg % 2].n

            emit_load2(0)
            emit_load2(1)
            for qg in range(8):
                def t(t, tc=tc, dc=dc, qg=qg):
                    t.wait_ge(c.dwS[qg % 2].h, load_mark2[qg])
                    if qg == 0:
                        t.wait_ge(c.ev.h, ev_base2)
                        t.wait_ge(c.sc.h, sc_base2)
                    sl = (qg % 2) * 4096
                    ins = None
                    for hl in range(8):
                        ht = qg * 8 + hl
                        for tt in range(4):
                            ins = t.matmul(
                                c.PS[tt][:, :],
                                hb_slice(ht, tt * 128, (tt + 1) * 128),
                                wb5[:, sl + hl * 512:sl + (hl + 1) * 512],
                                start=(ht == 0), stop=(ht == NHT - 1))
                    c.mm.inc(ins)
                on(c, "tensor", t)
                fill2[qg] = c.mm.n
                if qg + 2 < 8:
                    emit_load2(qg + 2)

            def v(v, tc=tc, dc=dc):
                v.wait_ge(c.mm.h, c.mm.n)
                ins = None
                for tt in range(4):
                    tg = tc * 4 + tt
                    ins = v.scalar_tensor_tensor(
                        va[:, tg, dc * 512:(dc + 1) * 512], c.PS[tt][:, :], 1.0,
                        xa[:, tg, dc * 512:(dc + 1) * 512],
                        ALU.mult, ALU.add,
                        accum_out=c.stat[:, tg * 4 + dc:tg * 4 + dc + 1])
                c.ev.inc(ins)
            on(c, "vector", v)

    ln_epilogue(c, VB, AR[0], LN_EPS, junk=c.junkB)
def _phase_c(c):
    nc = c.nc
    AR = c.AR
    XT, QB, KB, VB = AR[1], AR[2], AR[3], AR[4]
    xt = XT.ap().rearrange("p (k t) -> p k t", k=NDT)
    qb = QB.ap().rearrange("p (a d) -> p a d", a=NTT)
    kb = KB.ap().rearrange("p (a d) -> p a d", a=NTT)
    vb = VB.ap().rearrange("p (a d) -> p a d", a=NTT)
    wb = c.AR[5].ap().rearrange("p (k e) -> p k e", k=NDT)

    def t0(t):
        t.wait_ge(c.ev.h, c.ev.n)
        t.wait_ge(c.sc.h, c.sc.n)
    on(c, "tensor", t0)
    transpose_stream(c, AR[0], XT)

    it = 0
    for oc in range(12):
        def g(g, oc=oc):
            g.wait_ge(c.mm.h, c.mm.n)
            ins = g.dma_start(
                wb[:, :, 0:512],
                c.w_in.rearrange("(k p) e -> p k e", p=128)[:, :, oc * 512:(oc + 1) * 512])
            c.dW.inc(ins, 16)
            ins = g.dma_start(c.bias_ch[0:1, 0:512], c.b_in[0:1, oc * 512:(oc + 1) * 512])
            c.dW.inc(ins, 16)
        on(c, "gpsimd", g)

        for tt in range(NTT):
            bank = c.PS[it % 2]
            it += 1

            def t(t, oc=oc, tt=tt, bank=bank):
                t.wait_ge(c.dW.h, c.dW.n)
                t.wait_ge(c.ev.h, c.ev.n)
                ins = t.matmul(bank[:, :], c.ones_row[0:1, 0:128],
                               c.bias_ch[0:1, 0:512],
                               start=True, stop=False)
                for kt in range(NDT):
                    ins = t.matmul(bank[:, :], xt[:, kt, tt * 128:(tt + 1) * 128],
                                   wb[:, kt, 0:512],
                                   start=False, stop=(kt == NDT - 1))
                c.mm.inc(ins)
            on(c, "tensor", t)

            def v(v, oc=oc, tt=tt, bank=bank):
                v.wait_ge(c.mm.h, c.mm.n)
                if oc < 4:
                    dst = qb[:, tt, oc * 512:(oc + 1) * 512]
                elif oc < 8:
                    dst = kb[:, tt, (oc - 4) * 512:(oc - 3) * 512]
                else:
                    dst = vb[:, tt, (oc - 8) * 512:(oc - 7) * 512]
                ins = v.tensor_copy(dst, bank[:, :])
                c.ev.inc(ins)
            on(c, "vector", v)

    _tap(c, "qb", QB.ap())
    _tap(c, "kbt", KB.ap())
    _tap(c, "vbt", VB.ap())
    def s3(s):
        s.wait_ge(c.ev.h, c.ev.n)
        ins = None
        for tt in range(NTT):
            for h in range(NH):
                ins = s.activation(c.junkC[:, 0:128], kb[:, tt, h * 128:(h + 1) * 128],
                                   AF.Square, accum_out=c.kn2[:, tt * 16 + h:tt * 16 + h + 1])
        for tt in range(NTT):
            for h in range(NH):
                ins = s.activation(c.junkC[:, 0:128], qb[:, tt, h * 128:(h + 1) * 128],
                                   AF.Square, accum_out=c.qn2[:, tt * 16 + h:tt * 16 + h + 1])
        c.sc.inc(ins)
    on(c, "scalar", s3)

    for half in range(2):
        def t3(t, half=half):
            t.wait_ge(c.sc.h, c.sc.n)
            t.wait_ge(c.ev.h, c.ev.n)
            ins = None
            for j in range(4):
                tt = half * 4 + j
                ins = t.transpose(c.PS[4].ap()[0:16, j * 128:(j + 1) * 128],
                                  c.kn2[:, tt * 16:(tt + 1) * 16], c.identf[:, :])
                ins = t.transpose(c.PS[5].ap()[0:16, j * 128:(j + 1) * 128],
                                  c.qn2[:, tt * 16:(tt + 1) * 16], c.identf[:, :])
            c.mm.inc(ins)
        on(c, "tensor", t3)

        def v3(v, half=half):
            v.wait_ge(c.mm.h, c.mm.n)
            v.tensor_copy(c.knT[:, half * 512:(half + 1) * 512], c.PS[4].ap()[0:16, 0:512])
            ins = v.tensor_copy(c.qnT[:, half * 512:(half + 1) * 512], c.PS[5].ap()[0:16, 0:512])
            c.ev.inc(ins)
        on(c, "vector", v3)

    for src_tm, base in ((kb, 0), (vb, 2048), (qb, 4096)):
        pairs = []
        da = XT.ap().rearrange("p (k t) -> p k t", k=NDT)
        for h in range(NH):
            for tt in range(NTT):
                pairs.append((src_tm[:, tt, h * 128:(h + 1) * 128],
                              da[:, h, tt * 128:(tt + 1) * 128]))
        def vw(v):
            v.wait_ge(c.dS.h, c.dS.n)
        on(c, "vector", vw)
        transpose_into(c, pairs)

        def gdma(g, base=base):
            g.wait_ge(c.ev.h, c.ev.n)
            ins = None
            for h in range(NH):
                ins = g.dma_start(
                    c.agc_in[base + h * 128: base + (h + 1) * 128, :],
                    XT.ap().rearrange("p (k t) -> p k t", k=NDT)[:, h, :])
                c.dS.inc(ins, 16)
        on(c, "gpsimd", gdma)

    def gnrm(g):
        g.wait_ge(c.ev.h, c.ev.n)
        ins = g.dma_start(c.agc_in[6144:6160, :], c.knT)
        c.dS.inc(ins, 16)
        ins = g.dma_start(c.agc_in[6160:6176, :], c.qnT)
        c.dS.inc(ins, 16)
        g.wait_ge(c.dS.h, c.dS.n)
        ins = g.collective_compute(
            "AllGather", ALU.bypass,
            replica_groups=[[0, 1, 2, 3], [4, 5, 6, 7]],
            ins=[c.agc_in], outs=[c.agc_out])
        c.cc.inc(ins)
        g.wait_ge(c.cc.h, c.cc.n)
    on(c, "gpsimd", gnrm)

    KH = QB.ap()[:, 0:4096]
    VT = QB.ap()[:, 4096:8192]
    QH = QB.ap()[:, 8192:12288]
    VH = QB.ap()[:, 12288:16384]
    knrow = QB.ap()[0:1, 12288:16384]
    qnrow = QB.ap()[0:1, 12288:16384]
    att = KB.ap()
    agv = c.agc_out_h.ap().rearrange("r (c n) -> (r c) n", c=2)

    GCOL = [0]

    for hl in range(4):
        def gh(g, hl=hl, g0=GCOL[0]):
            g.wait_ge(c.ev.h, c.ev.n)
            g.wait_ge(c.sc.h, c.sc.n)
            ins = None
            for ti, dst in ((0, KH), (1, VH), (2, QH)):
                for qc in range(8):
                    col = g0 + ti * 8 + qc
                    ins = g.indirect_dma_start(
                        dst.rearrange("p (a n) -> p a n", n=512)[:, qc, :], None,
                        agv, IndirectOffsetOnAxis(c.gidx_sb[:, col:col + 1], 0))
                    c.dS.inc(ins, 16)
            g.wait_ge(c.dS.h, c.dS.n)
            ins = g.nop()
            c.gp.inc(ins)
        on(c, "gpsimd", gh)
        GCOL[0] += 24

        def tvt(t, hl=hl):
            t.wait_ge(c.dS.h, c.dS.n)
            t.wait_ge(c.ev.h, c.ev.n)
        on(c, "tensor", tvt)
        pairs = []
        for tkt in range(32):
            pairs.append((VH.rearrange("p (a n) -> p a n", n=128)[:, tkt, :],
                          VT.rearrange("p (a n) -> p a n", n=128)[:, tkt, :]))
        transpose_into(c, pairs)

        def gkn(g, hl=hl):
            g.wait_ge(c.ev.h, c.ev.n)
            with g.register(f"rkn{hl}") as reg:
                g.reg_load(reg, c.regt_sb[0:1, hl:hl + 1])
                ins = g.dma_start(
                    knrow.rearrange("o (b n) -> o b n", b=4),
                    bass.AP(c.agc_out_h, reg, [[0, 1], [RBLK * TLOC, 4], [1, TLOC]]))
                c.dS.inc(ins, 16)
            g.wait_ge(c.dS.h, c.dS.n)
            ins = g.nop()
            c.gp.inc(ins)
        on(c, "gpsimd", gkn)

        def vkm(v, hl=hl):
            v.wait_ge(c.gp.h, c.gp.n)
            ins = v.tensor_reduce(c.kmaxv[0:1, 0:1], knrow, AX.X, ALU.max)
            c.ev.inc(ins)
        on(c, "vector", vkm)

        def gqn(g, hl=hl):
            g.wait_ge(c.ev.h, c.ev.n)
            with g.register(f"rqn{hl}") as reg:
                g.reg_load(reg, c.regt_sb[0:1, 36 + hl:37 + hl])
                ins = g.dma_start(
                    qnrow.rearrange("o (b n) -> o b n", b=4),
                    bass.AP(c.agc_out_h, reg, [[0, 1], [RBLK * TLOC, 4], [1, TLOC]]))
                c.dS.inc(ins, 16)
            g.wait_ge(c.dS.h, c.dS.n)
            ins = g.nop()
            c.gp.inc(ins)
        on(c, "gpsimd", gqn)

        def tb(t, hl=hl):
            t.wait_ge(c.gp.h, c.gp.n)
            t.wait_ge(c.ev.h, c.ev.n)
            t.matmul(c.PS[5].ap()[:, 0:1], c.ones_row[0:1, 0:128],
                     c.kmaxv[0:1, 0:1], start=True, stop=True)
            ins = None
            for i in range(4):
                for sl in range(2):
                    qc = i if sl == 0 else 7 - i
                    for qt in range(4):
                        j = i * 8 + sl * 4 + qt
                        tj = qc * 4 + qt
                        ins = t.transpose(c.PT[0].ap()[:, 2 * tj:2 * tj + 1],
                                          qnrow[0:1, j * 128:(j + 1) * 128],
                                          c.ident[0:1, 0:1])
            c.mm.inc(ins)
        on(c, "tensor", tb)

        def vb2(v, hl=hl):
            v.wait_ge(c.mm.h, c.mm.n)
            v.tensor_copy(c.qnTh, c.PT[0].ap()[:, 0:64:2])
            v.drain()
            ins = v.tensor_scalar(c.biasH, c.qnTh, c.PS[5].ap()[:, 0:1], BSCALE,
                                  ALU.add, ALU.mult)
            c.ev.inc(ins)
        on(c, "vector", vb2)

        def vmsH(v):
            v.wait_ge(c.ev.h, c.ev.n)
            ins = v.memset(c.rsbH, 0.0)
            c.ev.inc(ins)
        on(c, "vector", vmsH)

        for qc in range(8):

            for ci in range(qc + 1):
                diag = ci == qc
                def ts(t, hl=hl, qc=qc, ci=ci, diag=diag):
                    t.wait_ge(c.sc.h, c.sc.n)
                    t.wait_ge(c.ev.h, c.ev.n)
                    ins = None
                    for qt in range(4):
                        w = (qt + 1) * 128 if diag else 512
                        ins = t.matmul(c.PS[qt].ap()[:, 0:w],
                                       QH[:, qc * 512 + qt * 128: qc * 512 + (qt + 1) * 128],
                                       KH[:, ci * 512: ci * 512 + w],
                                       start=True, stop=True)
                    c.mm.inc(ins)
                on(c, "tensor", ts)

                if diag:
                    def vtri(v, qc=qc):
                        v.wait_ge(c.mm.h, c.mm.n)
                        ins = None
                        for qt in range(4):
                            ins = v.tensor_tensor(
                                c.PS[qt].ap()[:, qt * 128:(qt + 1) * 128],
                                c.PS[qt].ap()[:, qt * 128:(qt + 1) * 128],
                                c.tri[:, :], ALU.add)
                        c.ev.inc(ins)
                    on(c, "vector", vtri)

                def se(s, hl=hl, qc=qc, ci=ci, diag=diag):
                    s.wait_ge(c.mm.h, c.mm.n)
                    if diag:
                        s.wait_ge(c.ev.h, c.ev.n)
                    ins = None
                    for qt in range(4):
                        w = (qt + 1) * 128 if diag else 512
                        ins = s.activation(
                            c.pb[:, qt * 512: qt * 512 + w],
                            c.PS[qt].ap()[:, 0:w], AF.Exp, scale=ISQD,
                            bias=c.biasH[:, qc * 4 + qt: qc * 4 + qt + 1],
                            accum_out=c.rsbH[:, (qc * 4 + qt) * 8 + ci:
                                             (qc * 4 + qt) * 8 + ci + 1])
                    c.sc.inc(ins)
                on(c, "scalar", se)

                tp_pairs = []
                for qt in range(4):
                    for tkt in range((qt + 1) if diag else 4):
                        tp_pairs.append(
                            (c.pb.rearrange("p (a n) -> p a n", n=128)[:, qt * 4 + tkt, :],
                             c.ptb.rearrange("p (a n) -> p a n", n=128)[:, tkt * 4 + qt, :]))

                def ttp(t, tp_pairs=tp_pairs):
                    t.wait_ge(c.sc.h, c.sc.n)
                    t.wait_ge(c.ev.h, c.ev.n)
                    ins = None
                    for i, (sp, dp) in enumerate(tp_pairs[:8]):
                        ins = t.transpose(c.PT[0].ap().rearrange("p (a n) -> p a n", n=128)[:, i, :], sp, c.ident[:, :])
                    c.mm.inc(ins)
                    if len(tp_pairs) > 8:
                        ins = None
                        for i, (sp, dp) in enumerate(tp_pairs[8:]):
                            ins = t.transpose(c.PT[1].ap().rearrange("p (a n) -> p a n", n=128)[:, i, :], sp, c.ident[:, :])
                        c.mm.inc(ins)
                on(c, "tensor", ttp)

                def vtp(v, tp_pairs=tp_pairs):
                    v.wait_ge(c.mm.h, c.mm.n)
                    ins = None
                    for i, (sp, dp) in enumerate(tp_pairs):
                        bank = c.PT[0] if i < 8 else c.PT[1]
                        ins = v.tensor_copy(
                            dp, bank.ap().rearrange("p (a n) -> p a n", n=128)[:, i % 8, :])
                    c.ev.inc(ins)
                on(c, "vector", vtp)

                def tpv(t, hl=hl, qc=qc, ci=ci, diag=diag):
                    t.wait_ge(c.ev.h, c.ev.n)
                    ins = None
                    for tkt in range(4):
                        off = tkt * 128 if diag else 0
                        ins = t.matmul(
                            c.PS[4].ap()[:, off:512],
                            VT[:, (ci * 4 + tkt) * 128:(ci * 4 + tkt + 1) * 128],
                            c.ptb[:, tkt * 512 + off: (tkt + 1) * 512],
                            start=(ci == 0 and tkt == 0),
                            stop=(diag and tkt == 3))
                    c.mm.inc(ins)
                on(c, "tensor", tpv)

            def vpv(v, hl=hl, qc=qc):
                v.wait_ge(c.mm.h, c.mm.n)
                ins = v.tensor_copy(
                    c.AR[4].ap().bitcast(F32)[:, qc * 512:(qc + 1) * 512],
                    c.PS[4].ap()[:, :])
                c.ev.inc(ins)
            on(c, "vector", vpv)

        def vrsH(v, hl=hl):
            v.wait_ge(c.sc.h, c.sc.n)
            v.tensor_reduce(c.rsumH,
                            c.rsbH.rearrange("p (a f) -> p a f", a=32),
                            AX.X, ALU.add)
            v.drain()
            ins = v.reciprocal(c.rsumH, c.rsumH)
            c.ev.inc(ins)
        on(c, "vector", vrsH)

        def trtH(t, hl=hl):
            t.wait_ge(c.ev.h, c.ev.n)
            ins = t.transpose(c.PS[5].ap()[0:32, 0:128], c.rsumH, c.identf[:, :])
            c.mm.inc(ins)
        on(c, "tensor", trtH)

        def vrtH(v, hl=hl):
            v.wait_ge(c.mm.h, c.mm.n)
            ins = v.tensor_copy(c.recipH, c.PS[5].ap()[0:32, 0:128])
            c.ev.inc(ins)
        on(c, "vector", vrtH)

        def grbH(g, hl=hl):
            g.wait_ge(c.ev.h, c.ev.n)
            ins = g.dma_start(c.rb, c.recipH)
            c.dS.inc(ins, 16)
            g.wait_ge(c.dS.h, c.dS.n)
        on(c, "gpsimd", grbH)

        rrm, bcm, fvm = {}, {}, {}
        for qc in range(8):
            sem = c.dsS[qc % 2]

            def grr(g, hl=hl, qc=qc, sem=sem):
                if qc >= 2:
                    g.wait_ge(c.mm.h, bcm[qc - 2])
                ins = g.dma_start(
                    c.rrow[0:1, (qc % 2) * 512:(qc % 2) * 512 + 512],
                    c.rb.rearrange("a n -> (a n)").unsqueeze(0)
                    [0:1, qc * 512:(qc + 1) * 512])
                sem.inc(ins, 16)
            on(c, "gpsimd", grr)
            rrm[qc] = sem.n

            def tbcH(t, hl=hl, qc=qc, sem=sem):
                t.wait_ge(sem.h, rrm[qc])
                if qc >= 1:
                    t.wait_ge(c.ev.h, fvm[qc - 1])
                ins = t.matmul(c.PS[5].ap()[:, :], c.ones_row[0:1, 0:128],
                               c.rrow[0:1, (qc % 2) * 512:(qc % 2) * 512 + 512],
                               start=True, stop=True)
                c.mm.inc(ins)
            on(c, "tensor", tbcH)
            bcm[qc] = c.mm.n

            def vfinH(v, hl=hl, qc=qc):
                v.wait_ge(c.mm.h, bcm[qc])
                ins = v.tensor_tensor(
                    att[:, hl * 4096 + qc * 512: hl * 4096 + (qc + 1) * 512],
                    c.AR[4].ap().bitcast(F32)[:, qc * 512:(qc + 1) * 512],
                    c.PS[5].ap()[:, :], ALU.mult)
                c.ev.inc(ins)
            on(c, "vector", vfinH)
            fvm[qc] = c.ev.n

    def g7(g):
        g.wait_ge(c.ev.h, c.ev.n)
        ins = None
        for hl in range(4):
            ins = g.dma_start(c.ag2_in[hl * 128:(hl + 1) * 128, :],
                              att[:, hl * 4096:(hl + 1) * 4096])
            c.dS.inc(ins, 16)
        g.wait_ge(c.dS.h, c.dS.n)
        ins = g.collective_compute(
            "AllGather", ALU.bypass,
            replica_groups=[[0, 1, 2, 3], [4, 5, 6, 7]],
            ins=[c.ag2_in], outs=[c.ag2_out])
        c.cc.inc(ins)
        g.wait_ge(c.cc.h, c.cc.n)
        ag2v = c.ag2_out_h.ap().rearrange("r (c n) -> (r c) n", c=8)
        ins = None
        for et in range(NDT):
            for ch in range(2):
                col = 96 + et * 2 + ch
                ins = g.indirect_dma_start(
                    VB.ap().rearrange("p (k t) -> p k t", k=NDT)[:, et, ch * 512:(ch + 1) * 512],
                    None, ag2v,
                    IndirectOffsetOnAxis(c.gidx_sb[:, col:col + 1], 0))
                c.dS.inc(ins, 16)
        g.wait_ge(c.dS.h, c.dS.n)
        ins = g.nop()
        c.gp.inc(ins)
    on(c, "gpsimd", g7)

    def tw7(t):
        t.wait_ge(c.gp.h, c.gp.n)
    on(c, "tensor", tw7)

    out_proj(c, VB, c.w_out, AR[0], QB, 1.0, c.b_out, wbuf2=XT.ap()[:, 0:8192])
    ln_epilogue(c, QB, AR[0], LN_EPS, junk=c.junkC)


def _dequant(c):
    nc = c.nc
    ar1i = c.AR[1].ap().bitcast(I32)
    nib = ar1i[:, 0:2048]
    bits = ar1i[:, 2048:4096]
    c1b = ar1i[:, 4096:6144]
    sgn = ar1i[:, 6144:8192]
    bitsf = c.AR[1].ap().bitcast(F32)[:, 2048:4096]
    wtile = c.AR[2].ap()[:, 0:2048]
    stg = c.AR[2].ap()[:, 2048:4096]

    for u in range(17):
        pk_src = (c.gup_sl[u * 128:(u + 1) * 128, :] if u < 11
                  else c.dnp_sl[(u - 11) * 128:(u - 10) * 128, :])
        sc_src = (c.gus_sl[u * 128:(u + 1) * 128, :] if u < 11
                  else c.dns_sl[(u - 11) * 128:(u - 10) * 128, :])

        def g(g, u=u, pk_src=pk_src, sc_src=sc_src):
            g.wait_ge(c.ev.h, c.ev.n)
            ins = g.dma_start(c.pk_sb[:, 0:256], pk_src)
            c.dS.inc(ins, 16)
            ins = g.dma_start(c.sc_sb[:, 0:16], sc_src)
            c.dS.inc(ins, 16)
        on(c, "gpsimd", g)

        def v(v, u=u):
            v.wait_ge(c.dS.h, c.dS.n)
            v.wait_ge(c.mm.h, c.mm.n)
            for j in range(8):
                v.tensor_scalar(nib[:, j:2048:8], c.pk_sb[:, 0:256], 4 * j, 15,
                                ALU.logical_shift_right, ALU.bitwise_and)
            v.drain()
            v.tensor_scalar(bits, nib, 7, 22, ALU.bitwise_and, ALU.logical_shift_left)
            v.drain()
            v.tensor_scalar(bits, bits, 0x3F000000, None, ALU.add)
            v.drain()
            v.tensor_scalar(c1b, bits, 0x3F400000, 0x400000, ALU.is_equal, ALU.mult)
            v.tensor_scalar(sgn, bits, 0x3F000000, 0x3F000000, ALU.is_equal, ALU.mult)
            v.drain()
            v.tensor_sub(bits, bits, c1b)
            v.drain()
            v.tensor_sub(bits, bits, sgn)
            v.tensor_scalar(sgn, nib, 8, 28, ALU.bitwise_and, ALU.logical_shift_left)
            v.drain()
            v.tensor_tensor(bits, bits, sgn, ALU.bitwise_or)
            v.drain()
            ins = v.tensor_tensor(
                wtile.rearrange("p (g k) -> p g k", k=128),
                bitsf.rearrange("p (g k) -> p g k", k=128),
                c.sc_sb[:, 0:16].unsqueeze(2).broadcast_to([128, 16, 128]),
                ALU.mult)
            c.ev.inc(ins)
        on(c, "vector", v)

        for half in range(2):
            lo, hi = half * 8, half * 8 + 8
            bank = c.PT[half % 2]

            def t(t, lo=lo, hi=hi, bank=bank):
                t.wait_ge(c.ev.h, c.ev.n)
                ins = None
                for i in range(lo, hi):
                    ins = t.transpose(bank.ap()[:, (i - lo) * 128:(i - lo + 1) * 128],
                                      wtile[:, i * 128:(i + 1) * 128], c.ident[:, :])
                c.mm.inc(ins)
            on(c, "tensor", t)

            def v2(v, lo=lo, hi=hi, bank=bank):
                v.wait_ge(c.mm.h, c.mm.n)
                v.wait_ge(c.dS.h, c.dS.n)
                ins = None
                for i in range(lo, hi):
                    ins = v.tensor_copy(stg[:, i * 128:(i + 1) * 128],
                                        bank.ap()[:, (i - lo) * 128:(i - lo + 1) * 128])
                c.ev.inc(ins)
            on(c, "vector", v2)

        def g2(g, u=u):
            g.wait_ge(c.ev.h, c.ev.n)
            ins = None
            for dt in range(16):
                ins = g.dma_start(
                    c.dq_in[u * D + dt * 128: u * D + (dt + 1) * 128, :],
                    stg[:, dt * 128:(dt + 1) * 128])
                c.dS.inc(ins, 16)
        on(c, "gpsimd", g2)

    def gag(g):
        g.wait_ge(c.dS.h, c.dS.n)
        ins = g.collective_compute(
            "AllGather", ALU.bypass, replica_groups=[list(range(8))],
            ins=[c.dq_in], outs=[c.dq_out])
        c.cc.inc(ins)
        g.wait_ge(c.cc.h, c.cc.n)
        ins = None
        for ug in range(48):
            ot, blk = ug // 3, ug % 3
            r, sl = ug // 6, ug % 6
            w = 1536 if blk == 2 else 2048
            src0 = (r * 17 + 11 + sl) * D * 128
            ins = g.dma_start(
                c.wdn_t[blk * 2048: blk * 2048 + w, ot * 128:(ot + 1) * 128],
                c.dq_out[(r * 17 + 11 + sl) * D: (r * 17 + 11 + sl) * D + w, :])
            c.dS.inc(ins, 16)
        g.wait_ge(c.dS.h, c.dS.n)
        ins = g.nop()
        c.gp.inc(ins)
    on(c, "gpsimd", gag)


def _gu_block(c, ot):
    r, sl = ot // 11, ot % 11
    base = (r * 17 + sl) * D
    return c.dq_out[base: base + D, :]


def _phase_d(c):
    nc = c.nc
    AR = c.AR
    XT, HB1, HB2, YB = AR[1], AR[2], AR[3], AR[4]
    xt = XT.ap().rearrange("p (k t) -> p k t", k=NDT)
    xa = AR[0].ap().rearrange("p (a d) -> p a d", a=NTT)
    ya = YB.ap().rearrange("p (a d) -> p a d", a=NTT)
    wb5 = c.AR[5].ap()

    _tap(c, "wdn", c.wdn_t, rows=IFF)
    def td0(t):
        t.wait_ge(c.ev.h, c.ev.n)
        t.wait_ge(c.sc.h, c.sc.n)
        t.wait_ge(c.dS.h, c.dS.n)
    on(c, "tensor", td0)
    transpose_stream(c, AR[0], XT)

    def hb_slice(it, a, b):
        if it < 32:
            return HB1.ap()[:, it * 512 + a: it * 512 + b]
        return HB2.ap()[:, (it - 32) * 512 + a: (it - 32) * 512 + b]

    NGRP = ((0, 15), (15, 15), (30, 14))

    for tc in range(2):
        load_mark, fillA, fillB, scm, evm = {}, {}, {}, {}, {}
        mm_base, ev_base, sc_base = c.mm.n, c.ev.n, c.sc.n

        def emit_load(ot, tc=tc):
            sem = c.dwS[ot % 2]

            def g(g, ot=ot, sem=sem):
                g.wait_ge(c.mm.h, fillB.get(ot - 2, mm_base))
                sl = (ot % 2) * 4096
                ins = g.dma_start(
                    wb5[:, sl:sl + 2048].rearrange("p (k e) -> p k e", e=128),
                    _gu_block(c, ot).rearrange("(k p) e -> p k e", p=128))
                sem.inc(ins, 16)
                ins = g.dma_start(
                    wb5[:, sl + 2048:sl + 4096].rearrange("p (k e) -> p k e", e=128),
                    _gu_block(c, NIT + ot).rearrange("(k p) e -> p k e", p=128))
                sem.inc(ins, 16)
            on(c, "gpsimd", g)
            load_mark[ot] = c.dwS[ot % 2].n

        emit_load(0)
        emit_load(1)
        for ot in range(NIT):
            bp = (ot % 2) * 2
            sl = (ot % 2) * 4096

            def t(t, tc=tc, ot=ot, bp=bp, sl=sl):
                t.wait_ge(c.dwS[ot % 2].h, load_mark[ot])
                t.wait_ge(c.sc.h, scm.get(ot - 2, sc_base))
                t.wait_ge(c.ev.h, evm.get(ot - 2, ev_base))
                ins = None
                for kt in range(NDT):
                    ins = t.matmul(c.PS[bp][:, :], wb5[:, sl + kt * 128:sl + (kt + 1) * 128],
                                   xt[:, kt, tc * 512:(tc + 1) * 512],
                                   start=(kt == 0), stop=(kt == NDT - 1))
                c.mm.inc(ins)
                fillA[ot] = c.mm.n
                ins = None
                for kt in range(NDT):
                    ins = t.matmul(c.PS[bp + 1][:, :],
                                   wb5[:, sl + 2048 + kt * 128:sl + 2048 + (kt + 1) * 128],
                                   xt[:, kt, tc * 512:(tc + 1) * 512],
                                   start=(kt == 0), stop=(kt == NDT - 1))
                c.mm.inc(ins)
            on(c, "tensor", t)
            fillB[ot] = c.mm.n

            def sil(s, ot=ot, bp=bp):
                s.wait_ge(c.mm.h, fillA[ot])
                ins = s.activation(c.sgB, c.PS[bp][:, :], AF.Silu)
                c.sc.inc(ins)
            on(c, "scalar", sil)
            scm[ot] = c.sc.n

            def v(v, ot=ot, bp=bp):
                v.wait_ge(c.sc.h, scm[ot])
                v.wait_ge(c.mm.h, fillB[ot])
                ins = v.tensor_tensor(hb_slice(ot, 0, 512), c.sgB,
                                      c.PS[bp + 1][:, :], ALU.mult)
                c.ev.inc(ins)
            on(c, "vector", v)
            evm[ot] = c.ev.n
            if ot + 2 < NIT:
                emit_load(ot + 2)

        if DEV_D_STOP == "gateup":
            continue
        GRP = [(0, 8), (8, 8), (16, 8), (24, 8), (32, 8), (40, 4)]
        for dc in range(4):
            load_mark2, fill2 = {}, {}
            mm_base2, ev_base2, sc_base2 = c.mm.n, c.ev.n, c.sc.n

            def emit_load2(qg, dc=dc):
                sem = c.dwS[qg % 2]

                def g(g, qg=qg, dc=dc, sem=sem):
                    g.wait_ge(c.mm.h, fill2.get(qg - 2, mm_base2))
                    i0, ng = GRP[qg]
                    sl = (qg % 2) * 4096
                    ins = g.dma_start(
                        wb5[:, sl:sl + ng * 512].rearrange("p (k e) -> p k e", e=512),
                        c.wdn_t[i0 * 128:(i0 + ng) * 128,
                                dc * 512:(dc + 1) * 512].rearrange("(k p) e -> p k e", p=128))
                    sem.inc(ins, 16)
                on(c, "gpsimd", g)
                load_mark2[qg] = c.dwS[qg % 2].n

            emit_load2(0)
            emit_load2(1)
            for qg in range(6):
                def t(t, tc=tc, dc=dc, qg=qg):
                    t.wait_ge(c.dwS[qg % 2].h, load_mark2[qg])
                    if qg == 0:
                        t.wait_ge(c.ev.h, ev_base2)
                        t.wait_ge(c.sc.h, sc_base2)
                    i0, ng = GRP[qg]
                    sl = (qg % 2) * 4096
                    ins = None
                    for hl in range(ng):
                        it = i0 + hl
                        for tt in range(4):
                            ins = t.matmul(
                                c.PS[tt][:, :],
                                hb_slice(it, tt * 128, (tt + 1) * 128),
                                wb5[:, sl + hl * 512:sl + (hl + 1) * 512],
                                start=(it == 0), stop=(it == NIT - 1))
                    c.mm.inc(ins)
                on(c, "tensor", t)
                fill2[qg] = c.mm.n
                if qg + 2 < 6:
                    emit_load2(qg + 2)

            def se(s, tc=tc, dc=dc):
                s.wait_ge(c.mm.h, c.mm.n)
                ins = None
                for tt in range(4):
                    ins = s.activation(c.junkD[:, 0:512], c.PS[tt][:, :], AF.Square,
                                       accum_out=c.stat2[:, 16 + tt * 4 + dc:17 + tt * 4 + dc])
                c.sc.inc(ins)
            on(c, "scalar", se)

            def ve(v, tc=tc, dc=dc):
                v.wait_ge(c.mm.h, c.mm.n)
                ins = None
                for tt in range(4):
                    tg = tc * 4 + tt
                    ins = v.tensor_copy(ya[:, tg, dc * 512:(dc + 1) * 512],
                                        c.PS[tt][:, :])
                c.ev.inc(ins)
            on(c, "vector", ve)

        if DEV_D_STOP == "down":
            continue
        def vr(v, tc=tc):
            v.wait_ge(c.sc.h, c.sc.n)
            v.wait_ge(c.ev.h, c.ev.n)
            v.tensor_reduce(c.stat2[:, 32:36],
                            c.stat2.ap().rearrange("p (a f) -> p a f", a=16)[:, 4:8, 0:4],
                            AX.X, ALU.add)
            v.drain()
            ins = v.tensor_scalar(c.stat2[:, 36:40], c.stat2[:, 32:36], 1.0 / D,
                                  RMS_EPS, ALU.mult, ALU.add)
            c.ev.inc(ins)
        on(c, "vector", vr)

        def sr(s, tc=tc):
            s.wait_ge(c.ev.h, c.ev.n)
            s.activation(c.stat2[:, 36:40], c.stat2[:, 36:40], AF.Ln)
            s.drain()
            ins = s.activation(c.stat2[:, 36:40], c.stat2[:, 36:40], AF.Exp, scale=-0.5)
            c.sc.inc(ins)
        on(c, "scalar", sr)

        def vv4(v, tc=tc):
            v.wait_ge(c.sc.h, c.sc.n)
            ins = None
            for tt in range(4):
                tg = tc * 4 + tt
                for q in range(4):
                    ins = v.scalar_tensor_tensor(
                        ya[:, tg, q * 512:(q + 1) * 512],
                        ya[:, tg, q * 512:(q + 1) * 512],
                        c.stat2[:, 36 + tt:37 + tt],
                        xa[:, tg, q * 512:(q + 1) * 512],
                        ALU.mult, ALU.add,
                        accum_out=c.stat[:, tg * 4 + q:tg * 4 + q + 1])
            c.ev.inc(ins)
        on(c, "vector", vv4)

    if DEV_D_STOP in ("gateup", "down"):
        def vfake(v):
            v.wait_ge(c.ev.h, c.ev.n)
            v.wait_ge(c.sc.h, c.sc.n)
            ins = v.tensor_copy(c.stgD, c.AR[0].ap().bitcast(F32)[:, 0:2048])
            c.ev.inc(ins)
        on(c, "vector", vfake)
        for tt in range(NTT):
            def syf(sy, tt=tt):
                sy.wait_ge(c.ev.h, c.ev.n)
                ins = sy.dma_start(c.y_out[tt * 128:(tt + 1) * 128, :], c.stgD)
                c.dO.inc(ins, 16)
            on(c, "sync", syf)
        return
    ln_epilogue_f32(c, YB, LN_EPS, junk=c.junkD)
    _tap(c, "statF", c.stat[:, :])
    _tap(c, "stat2F", c.stat2[:, :])



def _chunk_of(core):
    return core // 4, core % 4, 7 - core % 4


def _hadamard(n):
    i = np.arange(n)
    bits = (i[:, None] & i[None, :])
    pop = np.zeros((n, n), np.int64)
    x = bits.copy()
    while x.any():
        pop += x & 1
        x >>= 1
    return np.where(pop % 2 == 0, 1.0, -1.0).astype(np.float32)


def _host_consts(core):
    tri = np.triu(np.full((128, 128), -1e30, np.float32), k=1)
    cst = np.concatenate([tri, _hadamard(128)], axis=1).astype(np.float32)
    h64 = _hadamard(64)
    b, j0, j1 = _chunk_of(core)
    sel = np.zeros((16, 2), np.float32)
    for col, jj in enumerate((j0, j1)):
        for r in range(8):
            if r // 4 != b:
                continue
            for tc, jr in enumerate((r % 4, 7 - r % 4)):
                if jr < jj:
                    sel[2 * r + tc, col] = 1.0
    return cst, h64, sel


def _gidx_host(core):
    g, gr = core // 4, core % 4
    dh = np.arange(128)
    cols = []
    for hl in range(4):
        h = gr * 4 + hl
        for base in (0, 2048, 4096):
            for qc in range(8):
                i = qc if qc < 4 else 7 - qc
                sl = 0 if qc < 4 else 1
                cols.append((i * RBLK + base + h * 128 + dh) * 2 + sl)
    b, j0, j1 = _chunk_of(core)
    for et in range(NDT):
        for qc in (j0, j1):
            cols.append((et * 128 + dh) * 8 + qc)
    arr = np.zeros((128, 128), np.int32)
    a = np.stack(cols, 1).astype(np.int32)
    arr[:, :a.shape[1]] = a
    return arr


def _regt_host(core):
    g, gr = core // 4, core % 4
    r = np.zeros((1, 64), np.int32)
    for hl in range(4):
        h = gr * 4 + hl
        r[0, hl] = (6144 + h) * TLOC
        r[0, 36 + hl] = (6160 + h) * TLOC
    return r


def make_in_maps(inputs, phases=PHASES, x_override=None):
    x = x_override
    if x is None:
        x = np.ascontiguousarray(np.asarray(inputs["L0_attn_x"], np.float32).reshape(B * S, D))
    maps = []
    shared = {}
    if "A" in phases:
        shared["wq_t"] = np.ascontiguousarray(np.asarray(inputs["L0_attn_Wq"]).T)
        shared["wk_t"] = np.ascontiguousarray(np.asarray(inputs["L0_attn_Wk"]).T)
        shared["wv_t"] = np.ascontiguousarray(np.asarray(inputs["L0_attn_Wv"]).T)
        shared["wo_t"] = np.ascontiguousarray(np.asarray(inputs["L0_attn_Wo"]).T)
    if "B" in phases:
        shared["w1_t"] = np.ascontiguousarray(np.asarray(inputs["L0_ffn_W1"]).T)
        shared["w3_t"] = np.ascontiguousarray(np.asarray(inputs["L0_ffn_W3"]).T)
        shared["w2_t"] = np.ascontiguousarray(np.asarray(inputs["L0_ffn_W2"]).T)
    if "C" in phases:
        shared["w_in"] = np.ascontiguousarray(np.asarray(inputs["L1_attn_W_in"]))
        shared["b_in"] = np.ascontiguousarray(np.asarray(inputs["L1_attn_b_in"]).reshape(1, -1))
        shared["w_out"] = np.ascontiguousarray(np.asarray(inputs["L1_attn_W_out"]))
        shared["b_out"] = np.ascontiguousarray(np.asarray(inputs["L1_attn_b_out"]).reshape(1, -1))
    gu_packed = np.asarray(inputs["L1_ffn_gate_up_packed"]) if "D" in phases else None
    gu_scales = np.asarray(inputs["L1_ffn_gate_up_scales"]) if "D" in phases else None
    dn_packed = np.asarray(inputs["L1_ffn_down_packed"]) if "D" in phases else None
    dn_scales = np.asarray(inputs["L1_ffn_down_scales"]) if "D" in phases else None

    for core in range(8):
        b, j0, j1 = _chunk_of(core)
        rows = np.concatenate([
            x[b * S + j0 * CH: b * S + (j0 + 1) * CH],
            x[b * S + j1 * CH: b * S + (j1 + 1) * CH]], axis=0)
        cst, h64, sel = _host_consts(core)
        m = dict(shared)
        m.update({"x_in": np.ascontiguousarray(rows), "cst_f": cst, "cst2_f": h64})
        if "A" in phases:
            m["carry_sel"] = sel
        if "C" in phases:
            m["gidx"] = _gidx_host(core)
            m["regt"] = _regt_host(core)
        if "D" in phases:
            m["gup_sl"] = np.ascontiguousarray(gu_packed[core * 11 * 128:(core + 1) * 11 * 128])
            m["gus_sl"] = np.ascontiguousarray(gu_scales.T[core * 11 * 128:(core + 1) * 11 * 128])
            dnp_sl = np.zeros((6 * 128, 256), np.int32)
            dns_sl = np.zeros((6 * 128, 16), np.float16)
            for j in range(6):
                ug = 6 * core + j
                ot, blk = ug // 3, ug % 3
                wpc = 192 if blk == 2 else 256
                dnp_sl[j * 128:(j + 1) * 128, :wpc] = dn_packed[ot * 128:(ot + 1) * 128,
                                                               blk * 256: blk * 256 + wpc]
                dns_sl[j * 128:(j + 1) * 128, :wpc // 16] = \
                    dn_scales.T[ot * 128:(ot + 1) * 128, blk * 16: blk * 16 + wpc // 16]
            m["dnp_sl"] = dnp_sl
            m["dns_sl"] = dns_sl
        maps.append(m)
    return maps


def gather_out(res):
    out = np.zeros((B * S, D), np.float32)
    for core in range(8):
        b, j0, j1 = _chunk_of(core)
        r = np.asarray(res.results[core]["y_out"], np.float32)
        out[b * S + j0 * CH: b * S + (j0 + 1) * CH] = r[:CH]
        out[b * S + j1 * CH: b * S + (j1 + 1) * CH] = r[CH:]
    return out.reshape(B, S, D)


def kernel(**inputs):
    nc = build_program(PHASES)
    in_maps = make_in_maps(inputs, PHASES)
    res = run_bass_kernel_spmd(nc, in_maps, core_ids=list(range(8)))
    return gather_out(res)


# version 12 (final)
# speedup vs baseline: 1.7272x; 1.0360x over previous
import math
import numpy as np
import ml_dtypes
from contextlib import ExitStack

import concourse.bass as bass
import concourse.mybir as mybir
from concourse.bass_utils import run_bass_kernel_spmd
from concourse.bass_core_rust import IndirectOffsetOnAxis

F32 = mybir.dt.float32
BF16 = mybir.dt.bfloat16
I32 = mybir.dt.int32
F16 = mybir.dt.float16
AF = mybir.ActivationFunctionType
ALU = mybir.AluOpType
AX = mybir.AxisListType

B, S, D = 2, 4096, 2048
HFF = 8192
NH, DH = 16, 128
IFF = 5632
NCH = 8
CH = S // NCH
TLOC = 2 * CH
NTT = TLOC // 128
NDT = D // 128
DENOM = math.sqrt(D) / S
LN_EPS = 1e-5
RMS_EPS = 1e-6
ISQD = 1.0 / math.sqrt(DH)
BSCALE = -1.0 / (2.0 * math.sqrt(DH))
FWHT_SCALE = 1.0 / math.sqrt(HFF)

RBLK = 3 * D + 2 * NH
NIT = IFF // 128
NGU = 2 * NIT

PHASES = ("A", "B", "C", "D")
DEV_TAPS = []
DEV_DEQ_ONLY = False
DEV_D_STOP = "full"


class Sem:
    __slots__ = ("h", "n")

    def __init__(self, ctx, nc, name):
        self.h = ctx.enter_context(nc.semaphore(name))
        self.n = 0

    def inc(self, ins, k=1):
        ins.then_inc(self.h, k)
        self.n += k
        return ins


class Ctx:
    pass


def on(c, eng_name, f):
    getattr(c.block, eng_name)(f)


def build_program(phases=PHASES):
    nc = bass.Bass("TRN2", target_bir_lowering=False)
    c = Ctx()
    c.nc = nc
    c.phases = phases

    c.x_in = nc.dram_tensor("x_in", [TLOC, D], F32, kind="ExternalInput").ap()
    c.y_out = nc.dram_tensor("y_out", [TLOC, D], F32, kind="ExternalOutput").ap()

    if "A" in phases:
        c.wq_t = nc.dram_tensor("wq_t", [D, D], F32, kind="ExternalInput").ap()
        c.wk_t = nc.dram_tensor("wk_t", [D, D], F32, kind="ExternalInput").ap()
        c.wv_t = nc.dram_tensor("wv_t", [D, D], F32, kind="ExternalInput").ap()
        c.wo_t = nc.dram_tensor("wo_t", [D, D], F32, kind="ExternalInput").ap()
        c.carry_sel = nc.dram_tensor("carry_sel", [16, 2], F32, kind="ExternalInput").ap()
        c.ag1_in = nc.dram_tensor("ag1_in", [2, D], F32).ap()
        c.ag1_out = nc.dram_tensor("ag1_out", [16, D], F32).ap()
    if "B" in phases:
        c.w1_t = nc.dram_tensor("w1_t", [D, HFF], F32, kind="ExternalInput").ap()
        c.w3_t = nc.dram_tensor("w3_t", [D, HFF], F32, kind="ExternalInput").ap()
        c.w2_t = nc.dram_tensor("w2_t", [HFF, D], F32, kind="ExternalInput").ap()
        c.w2a = nc.dram_tensor("w2a", [HFF, D], BF16).ap()
        c.w2b = nc.dram_tensor("w2b", [HFF, D], BF16).ap()
    if "C" in phases:
        c.w_in = nc.dram_tensor("w_in", [D, 3 * D], F32, kind="ExternalInput").ap()
        c.b_in = nc.dram_tensor("b_in", [1, 3 * D], F32, kind="ExternalInput").ap()
        c.w_out = nc.dram_tensor("w_out", [D, D], F32, kind="ExternalInput").ap()
        c.b_out = nc.dram_tensor("b_out", [1, D], F32, kind="ExternalInput").ap()
        c.gidx = nc.dram_tensor("gidx", [128, 128], I32, kind="ExternalInput").ap()
        c.regt = nc.dram_tensor("regt", [1, 64], I32, kind="ExternalInput").ap()
        c.agc_in = nc.dram_tensor("agc_in", [RBLK, TLOC], BF16).ap()
        c.agc_out_h = nc.dram_tensor("agc_out", [4 * RBLK, TLOC], BF16)
        c.agc_out = c.agc_out_h.ap()
        c.ag2_in = nc.dram_tensor("ag2_in", [512, S], BF16).ap()
        c.ag2_out_h = nc.dram_tensor("ag2_out", [4 * 512, S], BF16)
        c.ag2_out = c.ag2_out_h.ap()
        c.rb = nc.dram_tensor("rb", [32, 128], F32).ap()
    if "D" in phases:
        c.gup_sl = nc.dram_tensor("gup_sl", [11 * 128, 256], I32, kind="ExternalInput").ap()
        c.gus_sl = nc.dram_tensor("gus_sl", [11 * 128, 16], F16, kind="ExternalInput").ap()
        c.dnp_sl = nc.dram_tensor("dnp_sl", [6 * 128, 256], I32, kind="ExternalInput").ap()
        c.dns_sl = nc.dram_tensor("dns_sl", [6 * 128, 16], F16, kind="ExternalInput").ap()
        c.dq_in = nc.dram_tensor("dq_in", [17 * D, 128], BF16).ap()
        c.dq_out_h = nc.dram_tensor("dq_out", [8 * 17 * D, 128], BF16)
        c.dq_out = c.dq_out_h.ap()
        c.wdn_t = nc.dram_tensor("wdn_t", [IFF, D], BF16).ap()

    c.cst_f = nc.dram_tensor("cst_f", [128, 256], F32, kind="ExternalInput").ap()
    c.cst2_f = nc.dram_tensor("cst2_f", [64, 64], F32, kind="ExternalInput").ap()

    with ExitStack() as ctx:
        c.ctx = ctx

        def sb(name, shape, dt):
            return ctx.enter_context(nc.sbuf_tensor(name, shape, dt))

        c.AR = [sb(f"AR{i}", [128, 16384], BF16) for i in range(5)]
        c.AR.append(sb("AR5", [128, 8192], BF16))
        c.SM = sb("SM", [128, 10752], BF16)
        c.ident = sb("ident", [128, 128], BF16)
        c.identf = sb("identf", [128, 128], F32)
        c.tri = sb("tri", [128, 128], F32)
        c.h128 = sb("h128", [128, 128], BF16)
        c.h64 = sb("h64", [64, 64], BF16)
        c.ones_row = sb("ones_row", [1, 512], BF16)
        c.ones_t = sb("ones_t", [128, 512], F32)
        c.stat = sb("stat", [128, 64], F32)
        c.stat2 = sb("stat2", [128, 64], F32)
        c.iob = sb("iob", [128, 96], I32)
        c.selS = sb("selS", [16, 2], F32)
        if "C" in phases:
            c.bias_ch = sb("bias_ch", [1, 1024], BF16)

        smf = c.SM.ap().bitcast(F32)
        smb = c.SM.ap()
        smi = c.SM.ap().bitcast(I32)
        def fv(off_b, n, r0=0, nr=128):
            return smf[r0:r0 + nr, off_b // 4: off_b // 4 + n]
        def bv(off_b, n, r0=0, nr=128):
            return smb[r0:r0 + nr, off_b // 2: off_b // 2 + n]
        def iv(off_b, n, r0=0, nr=128):
            return smi[r0:r0 + nr, off_b // 4: off_b // 4 + n]
        c.ssum = fv(0, 32)
        c.carT = fv(128, 32)
        c.rtmp = fv(256, 128, 0, 64)
        c.agS = fv(768, 2048, 0, 16)
        c.junkA = bv(8960, 2048)
        c.junkB = bv(0, 2048)
        c.sgB = bv(4096, 512)
        c.pb = bv(0, 2048)
        c.ptb = bv(4096, 2048)
        c.qn2 = fv(8192, 128)
        c.kn2 = fv(8704, 128)
        c.knT = bv(9216, 1024, 0, 16)
        c.qnT = bv(11264, 1024, 0, 16)
        c.biasH = fv(13312, 32)
        c.rsbC = fv(13440, 32)
        c.rsumC = fv(13568, 4)
        c.qnTh = fv(13632, 32)
        c.recipT = fv(13760, 128, 0, 4)
        c.rrow = bv(14272, 1024, 0, 1)
        c.rsbH = fv(17472, 256)
        c.rsumH = fv(18496, 32)
        c.recipH = fv(18688, 128, 0, 32)
        c.tmpPV = fv(15360, 512)
        c.kmaxv = bv(17408, 2, 0, 1)
        c.junkC = bv(0, 2048)
        if "C" in phases:
            c.regt_sb = sb("regt_sb", [1, 64], I32)
            c.gidx_sb = sb("gidx_sb", [128, 128], I32)
        c.pk_sb = iv(0, 704)
        c.sc_sb = fv(2816, 44)
        c.stgD = fv(4096, 2048)
        c.junkD = bv(12288, 2048)

        c.PS = [ctx.enter_context(nc.psum_tensor(f"PS{i}", [128, 512], F32))
                for i in range(6)]
        c.PT = [ctx.enter_context(nc.psum_tensor(f"PT{i}", [128, 1024], BF16))
                for i in range(2)]

        for nm in ["dX", "dW", "dS", "dO", "mm", "ev", "sc", "gp", "cc",
                   "dwA", "dwB", "dsA", "dsB"]:
            setattr(c, nm, Sem(ctx, nc, nm))
        c.dwS = [c.dwA, c.dwB]
        c.dsS = [c.dsA, c.dsB]

        c.block = ctx.enter_context(nc.Block())
        _emit(c)

    return nc


def _tap(c, name, src_ap, rows=128):
    if name not in DEV_TAPS:
        return
    dt_out = nc_dt = F32
    t = c.nc.dram_tensor("tap_" + name, [rows, src_ap.shape[-1] if hasattr(src_ap, 'shape') else 0], dt_out, kind="ExternalOutput").ap()

    def g(g):
        g.wait_ge(c.ev.h, c.ev.n)
        g.wait_ge(c.sc.h, c.sc.n)
        g.wait_ge(c.mm.h, c.mm.n)
        ins = g.dma_start(t, src_ap)
        c.dO.inc(ins, 16)
    on(c, "gpsimd", g)

    def vw(v):
        v.wait_ge(c.dO.h, c.dO.n)
    on(c, "vector", vw)

    def tw(t):
        t.wait_ge(c.dO.h, c.dO.n)
    on(c, "tensor", tw)


def _emit(c):
    _setup(c)
    if "A" in c.phases:
        _phase_a(c)
    if "D" in c.phases:
        _dequant(c)
    if "B" in c.phases:
        _w2_transform(c)
        _phase_b(c)
    if "C" in c.phases:
        _phase_c(c)
    if "D" in c.phases:
        _dequant_finish(c)
        if not DEV_DEQ_ONLY:
            _phase_d(c)
    if c.phases[-1] != "D" or DEV_DEQ_ONLY:
        _store_stream_dev(c)
    _final_wait(c)


def _setup(c):
    def g(g):
        ins = g.dma_start(c.AR[0].ap().rearrange("p (a d) -> p a d", a=NTT),
                          c.x_in.rearrange("(a p) d -> p a d", p=128))
        c.dX.inc(ins, 16)
        ins = g.dma_start(c.tri[:, :], c.cst_f[:, 0:128])
        c.dX.inc(ins, 16)
        ins = g.dma_start(c.h128[:, :], c.cst_f[:, 128:256])
        c.dX.inc(ins, 16)
        ins = g.dma_start(c.h64[:, :], c.cst2_f[:, :])
        c.dX.inc(ins, 16)
        if "A" in c.phases:
            ins = g.dma_start(c.selS[:, :], c.carry_sel)
            c.dX.inc(ins, 16)
        if "C" in c.phases:
            ins = g.dma_start(c.gidx_sb[:, :], c.gidx)
            c.dX.inc(ins, 16)
            ins = g.dma_start(c.regt_sb[:, :], c.regt)
            c.dX.inc(ins, 16)
        g.memset(c.ones_row[:, :], 1.0)
        g.memset(c.ones_t[:, :], 1.0)
        g.iota(c.iob[:, 64:65], [[0, 1]], channel_multiplier=1,
               allow_small_or_imprecise_dtypes=True)
        ins = g.iota(c.iob[:, 0:32], [[1, 32]], channel_multiplier=0,
                     allow_small_or_imprecise_dtypes=True)
        c.gp.inc(ins)
    on(c, "gpsimd", g)

    def v(v):
        v.wait_ge(c.gp.h, c.gp.n)
        for q in range(4):
            v.tensor_scalar(c.iob[:, 32:64], c.iob[:, 0:32], q * 32, None, ALU.add)
            v.drain()
            v.tensor_tensor(c.ident[:, q * 32:(q + 1) * 32], c.iob[:, 32:64],
                            c.iob[:, 64:65].broadcast_to([128, 32]), ALU.is_equal)
            v.drain()
        ins = v.tensor_copy(c.identf[:, :], c.ident[:, :])
        c.ev.inc(ins)
    on(c, "vector", v)


def _store_stream_dev(c):
    def g(g):
        g.wait_ge(c.ev.h, c.ev.n)
        g.wait_ge(c.sc.h, c.sc.n)
        ins = g.dma_start(c.y_out.rearrange("(a p) d -> p a d", p=128),
                          c.AR[0].ap().rearrange("p (a d) -> p a d", a=NTT))
        c.dO.inc(ins, 16)
    on(c, "gpsimd", g)


def _final_wait(c):
    def g(g):
        g.wait_ge(c.dO.h, c.dO.n)
    on(c, "gpsimd", g)



def pipe_loop(c, n, load_fn, fill_fn, evac_fn, pre_gate=None):
    fill_mark = [0] * (n + 4)
    evac_mark = [0] * (n + 4)
    load_mark = [0] * (n + 4)

    def emit_load(i):
        def g(g, i=i):
            if i >= 2:
                t_mark = fill_mark[i - 2]
                g.wait_ge(c.mm.h, t_mark)
            load_fn(g, i)
        on(c, "gpsimd", g)
        load_mark[i] = c.dW.n

    emit_load(0)
    if n > 1:
        emit_load(1)
    for i in range(n):
        bank = c.PS[i % 2]

        def t(t, i=i, bank=bank):
            t.wait_ge(c.dW.h, load_mark[i])
            if i >= 2:
                t.wait_ge(c.ev.h, evac_mark[i - 2])
            if pre_gate is not None and i == 0:
                pre_gate(t)
            fill_fn(t, i, bank)
        on(c, "tensor", t)
        fill_mark[i] = c.mm.n

        def v(v, i=i, bank=bank):
            v.wait_ge(c.mm.h, fill_mark[i])
            evac_fn(v, i, bank)
        on(c, "vector", v)
        evac_mark[i] = c.ev.n

        if i + 2 < n:
            emit_load(i + 2)


def transpose_into(c, pairs):
    n = len(pairs)
    groups = [(i, min(i + 8, n)) for i in range(0, n, 8)]
    for gi, (lo, hi) in enumerate(groups):
        bank = c.PT[gi % 2]

        def t(t, lo=lo, hi=hi, bank=bank):
            t.wait_ge(c.ev.h, c.ev.n)
            ins = None
            for i in range(lo, hi):
                ins = t.transpose(bank[:, (i - lo) * 128:(i - lo + 1) * 128],
                                  pairs[i][0], c.ident[:, :])
            c.mm.inc(ins)
        on(c, "tensor", t)

        def v(v, lo=lo, hi=hi, bank=bank):
            v.wait_ge(c.mm.h, c.mm.n)
            ins = None
            for i in range(lo, hi):
                ins = v.tensor_copy(pairs[i][1], bank[:, (i - lo) * 128:(i - lo + 1) * 128])
            c.ev.inc(ins)
        on(c, "vector", v)


def transpose_stream(c, src_arena, dst_arena):
    sa = src_arena.ap().rearrange("p (a d) -> p a d", a=NTT)
    da = dst_arena.ap().rearrange("p (k t) -> p k t", k=NDT)
    pairs = []
    for dt in range(NDT):
        for tt in range(NTT):
            pairs.append((sa[:, tt, dt * 128:(dt + 1) * 128],
                          da[:, dt, tt * 128:(tt + 1) * 128]))
    transpose_into(c, pairs)


def stream_proj(c, w_dram, dst_arena, w_cols=D, pre_gate=None):
    xt = c.AR[1].ap().rearrange("p (k t) -> p k t", k=NDT)
    wbh = [c.AR[5].ap()[:, 0:4096].rearrange("p (k e) -> p k e", k=NDT),
           c.AR[5].ap()[:, 4096:8192].rearrange("p (k e) -> p k e", k=NDT)]
    da = dst_arena.ap().rearrange("p (k t) -> p k t", k=NDT)
    nit = w_cols // 256

    load_mark, fill_mark, evac_mark = {}, {}, {}
    ev_base = c.ev.n
    mm_base = c.mm.n

    def emit_load(i):
        sem = c.dwS[i % 2]

        def g(g, i=i, sem=sem):
            g.wait_ge(c.mm.h, fill_mark.get(i - 2, mm_base))
            ins = g.dma_start(
                wbh[i % 2][:, :, 0:256],
                w_dram.rearrange("(k p) e -> p k e", p=128)[:, :, i * 256:(i + 1) * 256])
            sem.inc(ins, 16)
        on(c, "gpsimd", g)
        load_mark[i] = c.dwS[i % 2].n

    emit_load(0)
    if nit > 1:
        emit_load(1)
    si = 0
    for i in range(nit):
        for ei in range(2):
            for tc in range(2):
                bank = c.PS[si % 3]

                def t(t, si=si, i=i, ei=ei, tc=tc, bank=bank):
                    t.wait_ge(c.dwS[i % 2].h, load_mark[i])
                    t.wait_ge(c.ev.h, evac_mark.get(si - 3, ev_base))
                    if pre_gate is not None and si == 0:
                        pre_gate(t)
                    wb = wbh[i % 2]
                    ins = None
                    for kt in range(NDT):
                        ins = t.matmul(bank[:, :], wb[:, kt, ei * 128:(ei + 1) * 128],
                                       xt[:, kt, tc * 512:(tc + 1) * 512],
                                       start=(kt == 0), stop=(kt == NDT - 1))
                    c.mm.inc(ins)
                on(c, "tensor", t)
                fill_mark[i] = c.mm.n

                def v(v, si=si, i=i, ei=ei, tc=tc, bank=bank):
                    v.wait_ge(c.mm.h, fill_mark[i])
                    ins = v.tensor_copy(da[:, i * 2 + ei, tc * 512:(tc + 1) * 512],
                                        bank[:, :])
                    c.ev.inc(ins)
                on(c, "vector", v)
                evac_mark[si] = c.ev.n
                si += 1
        if i + 2 < nit:
            emit_load(i + 2)


def out_proj(c, stat_arena, w_dram, x_prev, vpre, scale, bias_dram,
             wbuf2=None, w_cols=D):
    sa = stat_arena.ap().rearrange("p (k t) -> p k t", k=NDT)
    bufs = [c.AR[5].ap()[:, 0:8192].rearrange("p (k e) -> p k e", k=NDT)]
    if wbuf2 is not None:
        bufs.append(wbuf2.rearrange("p (k e) -> p k e", k=NDT))
    xa = x_prev.ap().rearrange("p (a d) -> p a d", a=NTT)
    va = vpre.ap().rearrange("p (a d) -> p a d", a=NTT)
    nfc = w_cols // 512
    nb = len(bufs)
    load_mark, fill_mark, evac_mark = {}, {}, {}
    ev_base = c.ev.n
    mm_base = c.mm.n

    def emit_load(i):
        sem = c.dwS[i % 2]

        def g(g, i=i, sem=sem):
            g.wait_ge(c.mm.h, fill_mark.get(i - nb, mm_base))
            if i < nb:
                g.wait_ge(c.ev.h, ev_base)
            ins = g.dma_start(
                bufs[i % nb][:, :, 0:512],
                w_dram.rearrange("(k p) e -> p k e", p=128)[:, :, i * 512:(i + 1) * 512])
            sem.inc(ins, 16)
            if bias_dram is not None:
                ins = g.dma_start(c.bias_ch[0:1, (i % 2) * 512:(i % 2) * 512 + 512],
                                  bias_dram[0:1, i * 512:(i + 1) * 512])
                sem.inc(ins, 16)
        on(c, "gpsimd", g)
        load_mark[i] = c.dwS[i % 2].n

    for i in range(min(nb, nfc)):
        emit_load(i)

    si = 0
    for fc in range(nfc):
        for tt in range(NTT):
            bank = c.PS[si % 3]

            def t(t, fc=fc, tt=tt, bank=bank, si=si):
                t.wait_ge(c.dwS[fc % 2].h, load_mark[fc])
                t.wait_ge(c.ev.h, evac_mark.get(si - 3, ev_base))
                wb = bufs[fc % nb]
                ins = None
                first = True
                if bias_dram is not None:
                    ins = t.matmul(bank[:, :], c.ones_row[0:1, 0:128],
                                   c.bias_ch[0:1, (fc % 2) * 512:(fc % 2) * 512 + 512],
                                   start=True, stop=False)
                    first = False
                for et in range(NDT):
                    ins = t.matmul(bank[:, :], sa[:, et, tt * 128:(tt + 1) * 128],
                                   wb[:, et, 0:512],
                                   start=first, stop=(et == NDT - 1))
                    first = False
                c.mm.inc(ins)
            on(c, "tensor", t)
            fill_mark[fc] = c.mm.n

            def v(v, fc=fc, tt=tt, bank=bank, si=si):
                v.wait_ge(c.mm.h, fill_mark[fc])
                ins = v.scalar_tensor_tensor(
                    va[:, tt, fc * 512:(fc + 1) * 512], bank[:, :], scale,
                    xa[:, tt, fc * 512:(fc + 1) * 512],
                    ALU.mult, ALU.add,
                    accum_out=c.stat[:, tt * 4 + fc:tt * 4 + fc + 1])
                c.ev.inc(ins)
            on(c, "vector", v)
            evac_mark[si] = c.ev.n
            si += 1
        if fc + nb < nfc:
            emit_load(fc + nb)


def ln_epilogue(c, v_arena, out_arena, eps, junk=None):
    va = v_arena.ap().rearrange("p (a d) -> p a d", a=NTT)
    oa = out_arena.ap().rearrange("p (a d) -> p a d", a=NTT)

    def v1(v):
        v.wait_ge(c.ev.h, c.ev.n)
        v.drain()
        v.tensor_reduce(c.stat[:, 32:40],
                        c.stat.ap().rearrange("p (a f) -> p a f", a=16)[:, 0:8, 0:4],
                        AX.X, ALU.add)
        v.drain()
        v.tensor_scalar(c.stat[:, 32:40], c.stat[:, 32:40], 1.0 / D, None, ALU.mult)
        v.drain()
        ins = v.tensor_scalar(c.stat[:, 40:48], c.stat[:, 32:40], -1.0, None, ALU.mult)
        c.ev.inc(ins)
    on(c, "vector", v1)

    def s1(s):
        s.wait_ge(c.ev.h, c.ev.n)
        ins = None
        for tt in range(NTT):
            ins = s.activation(junk, va[:, tt, :],
                               AF.Square, bias=c.stat[:, 32 + tt:33 + tt], scale=-1.0,
                               accum_out=c.stat2[:, tt:tt + 1])
        c.sc.inc(ins)
    on(c, "scalar", s1)

    def v2(v):
        v.wait_ge(c.sc.h, c.sc.n)
        ins = v.tensor_scalar(c.stat[:, 48:56], c.stat2[:, 0:8], 1.0 / D, eps,
                              ALU.mult, ALU.add)
        c.ev.inc(ins)
    on(c, "vector", v2)

    def s2(s):
        s.wait_ge(c.ev.h, c.ev.n)
        s.activation(c.stat[:, 48:56], c.stat[:, 48:56], AF.Ln)
        s.drain()
        ins = s.activation(c.stat[:, 48:56], c.stat[:, 48:56], AF.Exp, scale=-0.5)
        c.sc.inc(ins)
    on(c, "scalar", s2)

    def v3(v):
        v.wait_ge(c.sc.h, c.sc.n)
        ins = None
        for tt in range(NTT):
            ins = v.tensor_scalar(oa[:, tt, :], va[:, tt, :],
                                  c.stat[:, 32 + tt:33 + tt],
                                  c.stat[:, 48 + tt:49 + tt],
                                  ALU.subtract, ALU.mult)
        c.ev.inc(ins)
    on(c, "vector", v3)


def ln_epilogue_f32(c, v_arena, eps, junk):
    va = v_arena.ap().rearrange("p (a d) -> p a d", a=NTT)

    def v1(v):
        v.wait_ge(c.ev.h, c.ev.n)
        v.drain()
        v.tensor_reduce(c.stat[:, 32:40],
                        c.stat.ap().rearrange("p (a f) -> p a f", a=16)[:, 0:8, 0:4],
                        AX.X, ALU.add)
        v.drain()
        v.tensor_scalar(c.stat[:, 32:40], c.stat[:, 32:40], 1.0 / D, None, ALU.mult)
        v.drain()
        ins = v.tensor_scalar(c.stat[:, 40:48], c.stat[:, 32:40], -1.0, None, ALU.mult)
        c.ev.inc(ins)
    on(c, "vector", v1)

    def s1(s):
        s.wait_ge(c.ev.h, c.ev.n)
        ins = None
        for tt in range(NTT):
            ins = s.activation(junk, va[:, tt, :],
                               AF.Square, bias=c.stat[:, 32 + tt:33 + tt], scale=-1.0,
                               accum_out=c.stat2[:, tt:tt + 1])
        c.sc.inc(ins)
    on(c, "scalar", s1)

    def v2(v):
        v.wait_ge(c.sc.h, c.sc.n)
        ins = v.tensor_scalar(c.stat[:, 48:56], c.stat2[:, 0:8], 1.0 / D, eps,
                              ALU.mult, ALU.add)
        c.ev.inc(ins)
    on(c, "vector", v2)

    def s2(s):
        s.wait_ge(c.ev.h, c.ev.n)
        s.activation(c.stat[:, 48:56], c.stat[:, 48:56], AF.Ln)
        s.drain()
        ins = s.activation(c.stat[:, 48:56], c.stat[:, 48:56], AF.Exp, scale=-0.5)
        c.sc.inc(ins)
    on(c, "scalar", s2)

    for tt in range(NTT):
        def v3(v, tt=tt):
            v.wait_ge(c.sc.h, c.sc.n)
            v.wait_ge(c.dO.h, c.dO.n)
            ins = v.tensor_scalar(c.stgD, va[:, tt, :],
                                  c.stat[:, 32 + tt:33 + tt],
                                  c.stat[:, 48 + tt:49 + tt],
                                  ALU.subtract, ALU.mult)
            c.ev.inc(ins)
        on(c, "vector", v3)

        def sy(sy, tt=tt):
            sy.wait_ge(c.ev.h, c.ev.n)
            ins = sy.dma_start(c.y_out[tt * 128:(tt + 1) * 128, :], c.stgD)
            c.dO.inc(ins, 16)
        on(c, "sync", sy)



def _phase_a(c):
    AR = c.AR
    XT, KT, VQ, UB = AR[1], AR[2], AR[3], AR[4]

    def t0(t):
        t.wait_ge(c.dX.h, c.dX.n)
        t.wait_ge(c.ev.h, c.ev.n)
    on(c, "tensor", t0)

    transpose_stream(c, AR[0], XT)

    stream_proj(c, c.wk_t, KT)
    stream_proj(c, c.wv_t, VQ)

    kt_a = KT.ap().rearrange("p (k t) -> p k t", k=NDT)
    vq_a = VQ.ap().rearrange("p (k t) -> p k t", k=NDT)
    ub_a = UB.ap().rearrange("p (k t) -> p k t", k=NDT)

    def vu(v):
        v.wait_ge(c.ev.h, c.ev.n)
        for tc in range(2):
            for et in range(NDT):
                v.tensor_tensor(ub_a[:, et, tc * 512:(tc + 1) * 512],
                                kt_a[:, et, tc * 512:(tc + 1) * 512],
                                vq_a[:, et, tc * 512:(tc + 1) * 512], ALU.mult)
        v.drain()
        ins = None
        for tc in range(2):
            for et in range(NDT):
                ins = v.tensor_reduce(c.ssum[:, tc * 16 + et:tc * 16 + et + 1],
                                      ub_a[:, et, tc * 512:(tc + 1) * 512],
                                      AX.X, ALU.add)
        c.ev.inc(ins)
    on(c, "vector", vu)

    stream_proj(c, c.wq_t, VQ)

    def tss(t):
        t.wait_ge(c.ev.h, c.ev.n)
        ins = t.transpose(c.PS[2].ap()[0:32, 0:128], c.ssum,
                          c.identf[:, :])
        c.mm.inc(ins)
    on(c, "tensor", tss)

    def vss(v):
        v.wait_ge(c.mm.h, c.mm.n)
        ins = v.tensor_copy(c.rtmp[0:32, :], c.PS[2].ap()[0:32, 0:128])
        c.ev.inc(ins)
    on(c, "vector", vss)

    def gss(g):
        g.wait_ge(c.ev.h, c.ev.n)
        for tc in range(2):
            ins = g.dma_start(
                c.ag1_in[tc:tc + 1, :].rearrange("o (e p) -> (o e) p", p=128),
                c.rtmp[tc * 16:(tc + 1) * 16, :])
            c.dS.inc(ins, 16)
        g.wait_ge(c.dS.h, c.dS.n)
        ins = g.collective_compute(
            "AllGather", ALU.bypass, replica_groups=[list(range(8))],
            ins=[c.ag1_in], outs=[c.ag1_out])
        c.cc.inc(ins)
        g.wait_ge(c.cc.h, c.cc.n)
        ins = g.dma_start(c.agS, c.ag1_out)
        c.dS.inc(ins, 16)
    on(c, "gpsimd", gss)

    def tcar(t):
        t.wait_ge(c.dS.h, c.dS.n)
        ins = None
        for et in range(NDT):
            ins = t.matmul(c.PS[3].ap()[0:128, et * 2:(et + 1) * 2],
                           c.agS[:, et * 128:(et + 1) * 128],
                           c.selS[0:16, 0:2], start=True, stop=True)
        c.mm.inc(ins)
    on(c, "tensor", tcar)

    def vcar(v):
        v.wait_ge(c.mm.h, c.mm.n)
        ins = v.tensor_copy(c.carT, c.PS[3].ap()[0:128, 0:32])
        c.ev.inc(ins)
    on(c, "vector", vcar)

    def vscan(v):
        v.drain()
        ins = None
        for et in range(NDT):
            for tc in range(2):
                ins = v.tensor_tensor_scan(
                    kt_a[:, et, tc * 512:(tc + 1) * 512],
                    c.ones_t[:, 0:512],
                    ub_a[:, et, tc * 512:(tc + 1) * 512],
                    c.carT[:, et * 2 + tc:et * 2 + tc + 1],
                    ALU.mult, ALU.add)
                v.drain()
        c.ev.inc(ins)
    on(c, "vector", vscan)

    _tap(c, "u", c.AR[4].ap())
    _tap(c, "kvt", c.AR[2].ap())
    _tap(c, "car", c.carT)
    _tap(c, "q", c.AR[3].ap())

    def vvh(v):
        v.drain()
        ins = None
        for et in range(NDT):
            ins = v.tensor_tensor(vq_a[:, et, :], vq_a[:, et, :], kt_a[:, et, :],
                                  ALU.mult)
        c.ev.inc(ins)
    on(c, "vector", vvh)

    _tap(c, "vhat", c.AR[3].ap())
    out_proj(c, VQ, c.wo_t, AR[0], UB, DENOM, None, wbuf2=KT.ap()[:, 0:8192])
    _tap(c, "vpre", c.AR[4].ap())
    _tap(c, "stat", c.stat[:, :])
    ln_epilogue(c, UB, AR[0], LN_EPS, junk=c.junkA)



def _w2_transform(c):
    wt = c.AR[5].ap()
    for stage in range(2):
        ntile = 64 if stage == 0 else 128
        npart = 128 if stage == 0 else 64
        stat_c = c.h128 if stage == 0 else c.h64
        scl = 1.0 if stage == 0 else FWHT_SCALE
        ldm, flm, evm, stm = {}, {}, {}, {}
        mm_base, ev_base = c.mm.n, c.ev.n
        st_base = [c.dsS[0].n, c.dsS[1].n]
        stage1_gate = [c.dsS[0].n, c.dsS[1].n]

        def emit_load(T, stage=stage):
            sem = c.dwS[T % 2]

            def g(g, T=T, stage=stage, sem=sem):
                g.wait_ge(c.mm.h, flm.get(T - 2, mm_base))
                if stage == 1 and T < 2:
                    g.wait_ge(c.dsS[0].h, stage1_gate[0])
                    g.wait_ge(c.dsS[1].h, stage1_gate[1])
                sl = (T % 2) * 2048
                if stage == 0:
                    ins = g.dma_start(wt[:, sl:sl + 2048],
                                      c.w2_t[T * 128:(T + 1) * 128, :])
                else:
                    ins = g.dma_start(
                        wt[0:64, sl:sl + 2048],
                        c.w2a.rearrange("(t p) d -> p t d", p=128)[T, :, :])
                sem.inc(ins, 16)
            on(c, "gpsimd", g)
            ldm[T] = c.dwS[T % 2].n

        emit_load(0)
        emit_load(1)
        evs = {}
        ss = 0
        for T in range(ntile):
            sl = (T % 2) * 2048
            so = 4096 + (T % 2) * 2048
            for jh in range(2):
                b0 = (ss % 3) * 2

                def t(t, T=T, jh=jh, b0=b0, ss=ss, stage=stage, npart=npart,
                      stat_c=stat_c, sl=sl):
                    if jh == 0:
                        t.wait_ge(c.dwS[T % 2].h, ldm[T])
                    t.wait_ge(c.ev.h, evs.get(ss - 3, ev_base))
                    ins = None
                    for j in (jh * 2, jh * 2 + 1):
                        ins = t.matmul(c.PS[b0 + j % 2].ap()[0:npart, :],
                                       stat_c[0:npart, 0:npart],
                                       wt[0:npart, sl + j * 512:sl + (j + 1) * 512],
                                       start=True, stop=True)
                    c.mm.inc(ins)
                on(c, "tensor", t)
                flm[(T, jh)] = c.mm.n

                def v(v, T=T, jh=jh, b0=b0, ss=ss, stage=stage, scl=scl,
                      npart=npart, so=so):
                    v.wait_ge(c.mm.h, flm[(T, jh)])
                    if jh == 0:
                        v.wait_ge(c.dsS[T % 2].h, stm.get(T - 2, st_base[T % 2]))
                    ins = None
                    for j in (jh * 2, jh * 2 + 1):
                        ins = v.tensor_scalar(wt[0:npart, so + j * 512:so + (j + 1) * 512],
                                              c.PS[b0 + j % 2].ap()[0:npart, :], scl, None,
                                              ALU.mult)
                    c.ev.inc(ins)
                on(c, "vector", v)
                evs[ss] = c.ev.n
                ss += 1
            flm[T] = c.mm.n
            evm[T] = c.ev.n

            sem2 = c.dsS[T % 2]

            def g2(g, T=T, stage=stage, npart=npart, so=so, sem2=sem2):
                g.wait_ge(c.ev.h, evm[T])
                if stage == 0:
                    ins = g.dma_start(c.w2a[T * 128:(T + 1) * 128, :],
                                      wt[:, so:so + 2048])
                else:
                    ins = g.dma_start(
                        c.w2b.rearrange("(t p) d -> p t d", p=128)[T, :, :],
                        wt[0:64, so:so + 2048])
                sem2.inc(ins, 16)
            on(c, "gpsimd", g2)
            stm[T] = c.dsS[T % 2].n
            if T + 2 < ntile:
                emit_load(T + 2)
def _phase_b(c):
    _tap(c, "w2a", c.w2a, rows=HFF)
    _tap(c, "w2b", c.w2b, rows=HFF)
    AR = c.AR
    XT, HB1, HB2, VB = AR[1], AR[2], AR[3], AR[4]
    xt = XT.ap().rearrange("p (k t) -> p k t", k=NDT)
    xa = AR[0].ap().rearrange("p (a d) -> p a d", a=NTT)
    va = VB.ap().rearrange("p (a d) -> p a d", a=NTT)
    wb5 = c.AR[5].ap()

    def tb0(t):
        t.wait_ge(c.ev.h, c.ev.n)
        t.wait_ge(c.sc.h, c.sc.n)
    on(c, "tensor", tb0)
    transpose_stream(c, AR[0], XT)

    NHT = HFF // 128

    def hb_slice(ht, a, b):
        if ht < 32:
            return HB1.ap()[:, ht * 512 + a: ht * 512 + b]
        return HB2.ap()[:, (ht - 32) * 512 + a: (ht - 32) * 512 + b]

    for tc in range(2):
        load_mark, fillA, fillB, scm, evm = {}, {}, {}, {}, {}
        mm_base, ev_base, sc_base = c.mm.n, c.ev.n, c.sc.n

        def emit_load(ht, tc=tc):
            sem = c.dwS[ht % 2]

            def g(g, ht=ht, sem=sem):
                g.wait_ge(c.mm.h, fillB.get(ht - 2, mm_base))
                sl = (ht % 2) * 4096
                ins = g.dma_start(
                    wb5[:, sl:sl + 2048].rearrange("p (k e) -> p k e", e=128),
                    c.w1_t.rearrange("(k p) e -> p k e", p=128)[:, :, ht * 128:(ht + 1) * 128])
                sem.inc(ins, 16)
                ins = g.dma_start(
                    wb5[:, sl + 2048:sl + 4096].rearrange("p (k e) -> p k e", e=128),
                    c.w3_t.rearrange("(k p) e -> p k e", p=128)[:, :, ht * 128:(ht + 1) * 128])
                sem.inc(ins, 16)
            on(c, "gpsimd", g)
            load_mark[ht] = c.dwS[ht % 2].n

        emit_load(0)
        emit_load(1)
        for ht in range(NHT):
            bp = (ht % 2) * 2
            sl = (ht % 2) * 4096

            def t(t, tc=tc, ht=ht, bp=bp, sl=sl):
                t.wait_ge(c.dwS[ht % 2].h, load_mark[ht])
                t.wait_ge(c.sc.h, scm.get(ht - 2, sc_base))
                t.wait_ge(c.ev.h, evm.get(ht - 2, ev_base))
                ins = None
                for kt in range(NDT):
                    ins = t.matmul(c.PS[bp][:, :], wb5[:, sl + kt * 128:sl + (kt + 1) * 128],
                                   xt[:, kt, tc * 512:(tc + 1) * 512],
                                   start=(kt == 0), stop=(kt == NDT - 1))
                c.mm.inc(ins)
                fillA[ht] = c.mm.n
                ins = None
                for kt in range(NDT):
                    ins = t.matmul(c.PS[bp + 1][:, :],
                                   wb5[:, sl + 2048 + kt * 128:sl + 2048 + (kt + 1) * 128],
                                   xt[:, kt, tc * 512:(tc + 1) * 512],
                                   start=(kt == 0), stop=(kt == NDT - 1))
                c.mm.inc(ins)
            on(c, "tensor", t)
            fillB[ht] = c.mm.n
            fillA[ht] = fillB[ht] - 1

            def sil(s, ht=ht, bp=bp):
                s.wait_ge(c.mm.h, fillA[ht])
                ins = s.activation(c.sgB, c.PS[bp][:, :], AF.Silu)
                c.sc.inc(ins)
            on(c, "scalar", sil)
            scm[ht] = c.sc.n

            def v(v, tc=tc, ht=ht, bp=bp):
                v.wait_ge(c.sc.h, scm[ht])
                v.wait_ge(c.mm.h, fillB[ht])
                ins = v.tensor_tensor(hb_slice(ht, 0, 512), c.sgB,
                                      c.PS[bp + 1][:, :], ALU.mult)
                c.ev.inc(ins)
            on(c, "vector", v)
            evm[ht] = c.ev.n
            if ht + 2 < NHT:
                emit_load(ht + 2)

        for dc in range(4):
            load_mark2, fill2 = {}, {}
            mm_base2, ev_base2, sc_base2 = c.mm.n, c.ev.n, c.sc.n

            def emit_load2(qg, dc=dc):
                sem = c.dwS[qg % 2]

                def g(g, qg=qg, dc=dc, sem=sem):
                    g.wait_ge(c.mm.h, fill2.get(qg - 2, mm_base2))
                    if qg < 2 and dc == 0:
                        g.wait_ge(c.dsS[0].h, c.dsS[0].n)
                        g.wait_ge(c.dsS[1].h, c.dsS[1].n)
                    sl = (qg % 2) * 4096
                    ins = g.dma_start(
                        wb5[:, sl:sl + 4096].rearrange("p (k e) -> p k e", e=512),
                        c.w2b.rearrange("(k p) e -> p k e", p=128)
                        [:, qg * 8:(qg + 1) * 8, dc * 512:(dc + 1) * 512])
                    sem.inc(ins, 16)
                on(c, "gpsimd", g)
                load_mark2[qg] = c.dwS[qg % 2].n

            emit_load2(0)
            emit_load2(1)
            for qg in range(8):
                def t(t, tc=tc, dc=dc, qg=qg):
                    t.wait_ge(c.dwS[qg % 2].h, load_mark2[qg])
                    if qg == 0:
                        t.wait_ge(c.ev.h, ev_base2)
                        t.wait_ge(c.sc.h, sc_base2)
                    sl = (qg % 2) * 4096
                    ins = None
                    for hl in range(8):
                        ht = qg * 8 + hl
                        for tt in range(4):
                            ins = t.matmul(
                                c.PS[tt][:, :],
                                hb_slice(ht, tt * 128, (tt + 1) * 128),
                                wb5[:, sl + hl * 512:sl + (hl + 1) * 512],
                                start=(ht == 0), stop=(ht == NHT - 1))
                    c.mm.inc(ins)
                on(c, "tensor", t)
                fill2[qg] = c.mm.n
                if qg + 2 < 8:
                    emit_load2(qg + 2)

            def v(v, tc=tc, dc=dc):
                v.wait_ge(c.mm.h, c.mm.n)
                ins = None
                for tt in range(4):
                    tg = tc * 4 + tt
                    ins = v.scalar_tensor_tensor(
                        va[:, tg, dc * 512:(dc + 1) * 512], c.PS[tt][:, :], 1.0,
                        xa[:, tg, dc * 512:(dc + 1) * 512],
                        ALU.mult, ALU.add,
                        accum_out=c.stat[:, tg * 4 + dc:tg * 4 + dc + 1])
                c.ev.inc(ins)
            on(c, "vector", v)

    ln_epilogue(c, VB, AR[0], LN_EPS, junk=c.junkB)
def _phase_c(c):
    nc = c.nc
    AR = c.AR
    XT, QB, KB, VB = AR[1], AR[2], AR[3], AR[4]
    xt = XT.ap().rearrange("p (k t) -> p k t", k=NDT)
    qb = QB.ap().rearrange("p (a d) -> p a d", a=NTT)
    kb = KB.ap().rearrange("p (a d) -> p a d", a=NTT)
    vb = VB.ap().rearrange("p (a d) -> p a d", a=NTT)
    wb = c.AR[5].ap().rearrange("p (k e) -> p k e", k=NDT)

    def t0(t):
        t.wait_ge(c.ev.h, c.ev.n)
        t.wait_ge(c.sc.h, c.sc.n)
    on(c, "tensor", t0)
    transpose_stream(c, AR[0], XT)

    it = 0
    for oc in range(12):
        def g(g, oc=oc):
            g.wait_ge(c.mm.h, c.mm.n)
            ins = g.dma_start(
                wb[:, :, 0:512],
                c.w_in.rearrange("(k p) e -> p k e", p=128)[:, :, oc * 512:(oc + 1) * 512])
            c.dW.inc(ins, 16)
            ins = g.dma_start(c.bias_ch[0:1, 0:512], c.b_in[0:1, oc * 512:(oc + 1) * 512])
            c.dW.inc(ins, 16)
        on(c, "gpsimd", g)

        for tt in range(NTT):
            bank = c.PS[it % 2]
            it += 1

            def t(t, oc=oc, tt=tt, bank=bank):
                t.wait_ge(c.dW.h, c.dW.n)
                t.wait_ge(c.ev.h, c.ev.n)
                ins = t.matmul(bank[:, :], c.ones_row[0:1, 0:128],
                               c.bias_ch[0:1, 0:512],
                               start=True, stop=False)
                for kt in range(NDT):
                    ins = t.matmul(bank[:, :], xt[:, kt, tt * 128:(tt + 1) * 128],
                                   wb[:, kt, 0:512],
                                   start=False, stop=(kt == NDT - 1))
                c.mm.inc(ins)
            on(c, "tensor", t)

            def v(v, oc=oc, tt=tt, bank=bank):
                v.wait_ge(c.mm.h, c.mm.n)
                if oc < 4:
                    dst = qb[:, tt, oc * 512:(oc + 1) * 512]
                elif oc < 8:
                    dst = kb[:, tt, (oc - 4) * 512:(oc - 3) * 512]
                else:
                    dst = vb[:, tt, (oc - 8) * 512:(oc - 7) * 512]
                ins = v.tensor_copy(dst, bank[:, :])
                c.ev.inc(ins)
            on(c, "vector", v)

    _tap(c, "qb", QB.ap())
    _tap(c, "kbt", KB.ap())
    _tap(c, "vbt", VB.ap())
    def s3(s):
        s.wait_ge(c.ev.h, c.ev.n)
        ins = None
        for tt in range(NTT):
            for h in range(NH):
                ins = s.activation(c.junkC[:, 0:128], kb[:, tt, h * 128:(h + 1) * 128],
                                   AF.Square, accum_out=c.kn2[:, tt * 16 + h:tt * 16 + h + 1])
        for tt in range(NTT):
            for h in range(NH):
                ins = s.activation(c.junkC[:, 0:128], qb[:, tt, h * 128:(h + 1) * 128],
                                   AF.Square, accum_out=c.qn2[:, tt * 16 + h:tt * 16 + h + 1])
        c.sc.inc(ins)
    on(c, "scalar", s3)

    for half in range(2):
        def t3(t, half=half):
            t.wait_ge(c.sc.h, c.sc.n)
            t.wait_ge(c.ev.h, c.ev.n)
            ins = None
            for j in range(4):
                tt = half * 4 + j
                ins = t.transpose(c.PS[4].ap()[0:16, j * 128:(j + 1) * 128],
                                  c.kn2[:, tt * 16:(tt + 1) * 16], c.identf[:, :])
                ins = t.transpose(c.PS[5].ap()[0:16, j * 128:(j + 1) * 128],
                                  c.qn2[:, tt * 16:(tt + 1) * 16], c.identf[:, :])
            c.mm.inc(ins)
        on(c, "tensor", t3)

        def v3(v, half=half):
            v.wait_ge(c.mm.h, c.mm.n)
            v.tensor_copy(c.knT[:, half * 512:(half + 1) * 512], c.PS[4].ap()[0:16, 0:512])
            ins = v.tensor_copy(c.qnT[:, half * 512:(half + 1) * 512], c.PS[5].ap()[0:16, 0:512])
            c.ev.inc(ins)
        on(c, "vector", v3)

    for src_tm, base in ((kb, 0), (vb, 2048), (qb, 4096)):
        pairs = []
        da = XT.ap().rearrange("p (k t) -> p k t", k=NDT)
        for h in range(NH):
            for tt in range(NTT):
                pairs.append((src_tm[:, tt, h * 128:(h + 1) * 128],
                              da[:, h, tt * 128:(tt + 1) * 128]))
        def vw(v):
            v.wait_ge(c.dS.h, c.dS.n)
        on(c, "vector", vw)
        transpose_into(c, pairs)

        def gdma(g, base=base):
            g.wait_ge(c.ev.h, c.ev.n)
            ins = None
            for h in range(NH):
                ins = g.dma_start(
                    c.agc_in[base + h * 128: base + (h + 1) * 128, :],
                    XT.ap().rearrange("p (k t) -> p k t", k=NDT)[:, h, :])
                c.dS.inc(ins, 16)
        on(c, "gpsimd", gdma)

    def gnrm(g):
        g.wait_ge(c.ev.h, c.ev.n)
        ins = g.dma_start(c.agc_in[6144:6160, :], c.knT)
        c.dS.inc(ins, 16)
        ins = g.dma_start(c.agc_in[6160:6176, :], c.qnT)
        c.dS.inc(ins, 16)
        g.wait_ge(c.dS.h, c.dS.n)
        ins = g.collective_compute(
            "AllGather", ALU.bypass,
            replica_groups=[[0, 1, 2, 3], [4, 5, 6, 7]],
            ins=[c.agc_in], outs=[c.agc_out])
        c.cc.inc(ins)
        g.wait_ge(c.cc.h, c.cc.n)
    on(c, "gpsimd", gnrm)

    KH = QB.ap()[:, 0:4096]
    VT = QB.ap()[:, 4096:8192]
    QH = QB.ap()[:, 8192:12288]
    VH = QB.ap()[:, 12288:16384]
    knrow = QB.ap()[0:1, 12288:16384]
    qnrow = QB.ap()[0:1, 12288:16384]
    att = KB.ap()
    agv = c.agc_out_h.ap().rearrange("r (c n) -> (r c) n", c=2)

    GCOL = [0]

    for hl in range(4):
        def gh(g, hl=hl, g0=GCOL[0]):
            g.wait_ge(c.ev.h, c.ev.n)
            g.wait_ge(c.sc.h, c.sc.n)
            ins = None
            for ti, dst in ((0, KH), (1, VH), (2, QH)):
                for qc in range(8):
                    col = g0 + ti * 8 + qc
                    ins = g.indirect_dma_start(
                        dst.rearrange("p (a n) -> p a n", n=512)[:, qc, :], None,
                        agv, IndirectOffsetOnAxis(c.gidx_sb[:, col:col + 1], 0))
                    c.dS.inc(ins, 16)
            g.wait_ge(c.dS.h, c.dS.n)
            ins = g.nop()
            c.gp.inc(ins)
        on(c, "gpsimd", gh)
        GCOL[0] += 24

        def tvt(t, hl=hl):
            t.wait_ge(c.dS.h, c.dS.n)
            t.wait_ge(c.ev.h, c.ev.n)
        on(c, "tensor", tvt)
        pairs = []
        for tkt in range(32):
            pairs.append((VH.rearrange("p (a n) -> p a n", n=128)[:, tkt, :],
                          VT.rearrange("p (a n) -> p a n", n=128)[:, tkt, :]))
        transpose_into(c, pairs)

        def gkn(g, hl=hl):
            g.wait_ge(c.ev.h, c.ev.n)
            with g.register(f"rkn{hl}") as reg:
                g.reg_load(reg, c.regt_sb[0:1, hl:hl + 1])
                ins = g.dma_start(
                    knrow.rearrange("o (b n) -> o b n", b=4),
                    bass.AP(c.agc_out_h, reg, [[0, 1], [RBLK * TLOC, 4], [1, TLOC]]))
                c.dS.inc(ins, 16)
            g.wait_ge(c.dS.h, c.dS.n)
            ins = g.nop()
            c.gp.inc(ins)
        on(c, "gpsimd", gkn)

        def vkm(v, hl=hl):
            v.wait_ge(c.gp.h, c.gp.n)
            ins = v.tensor_reduce(c.kmaxv[0:1, 0:1], knrow, AX.X, ALU.max)
            c.ev.inc(ins)
        on(c, "vector", vkm)

        def gqn(g, hl=hl):
            g.wait_ge(c.ev.h, c.ev.n)
            with g.register(f"rqn{hl}") as reg:
                g.reg_load(reg, c.regt_sb[0:1, 36 + hl:37 + hl])
                ins = g.dma_start(
                    qnrow.rearrange("o (b n) -> o b n", b=4),
                    bass.AP(c.agc_out_h, reg, [[0, 1], [RBLK * TLOC, 4], [1, TLOC]]))
                c.dS.inc(ins, 16)
            g.wait_ge(c.dS.h, c.dS.n)
            ins = g.nop()
            c.gp.inc(ins)
        on(c, "gpsimd", gqn)

        def tb(t, hl=hl):
            t.wait_ge(c.gp.h, c.gp.n)
            t.wait_ge(c.ev.h, c.ev.n)
            t.matmul(c.PS[5].ap()[:, 0:1], c.ones_row[0:1, 0:128],
                     c.kmaxv[0:1, 0:1], start=True, stop=True)
            ins = None
            for i in range(4):
                for sl in range(2):
                    qc = i if sl == 0 else 7 - i
                    for qt in range(4):
                        j = i * 8 + sl * 4 + qt
                        tj = qc * 4 + qt
                        ins = t.transpose(c.PT[0].ap()[:, 2 * tj:2 * tj + 1],
                                          qnrow[0:1, j * 128:(j + 1) * 128],
                                          c.ident[0:1, 0:1])
            c.mm.inc(ins)
        on(c, "tensor", tb)

        def vb2(v, hl=hl):
            v.wait_ge(c.mm.h, c.mm.n)
            v.tensor_copy(c.qnTh, c.PT[0].ap()[:, 0:64:2])
            v.drain()
            ins = v.tensor_scalar(c.biasH, c.qnTh, c.PS[5].ap()[:, 0:1], BSCALE,
                                  ALU.add, ALU.mult)
            c.ev.inc(ins)
        on(c, "vector", vb2)

        def vmsH(v):
            v.wait_ge(c.ev.h, c.ev.n)
            ins = v.memset(c.rsbH, 0.0)
            c.ev.inc(ins)
        on(c, "vector", vmsH)

        for qc in range(8):

            for ci in range(qc + 1):
                diag = ci == qc
                def ts(t, hl=hl, qc=qc, ci=ci, diag=diag):
                    t.wait_ge(c.sc.h, c.sc.n)
                    t.wait_ge(c.ev.h, c.ev.n)
                    ins = None
                    for qt in range(4):
                        w = (qt + 1) * 128 if diag else 512
                        ins = t.matmul(c.PS[qt].ap()[:, 0:w],
                                       QH[:, qc * 512 + qt * 128: qc * 512 + (qt + 1) * 128],
                                       KH[:, ci * 512: ci * 512 + w],
                                       start=True, stop=True)
                    c.mm.inc(ins)
                on(c, "tensor", ts)

                if diag:
                    def vtri(v, qc=qc):
                        v.wait_ge(c.mm.h, c.mm.n)
                        ins = None
                        for qt in range(4):
                            ins = v.tensor_tensor(
                                c.PS[qt].ap()[:, qt * 128:(qt + 1) * 128],
                                c.PS[qt].ap()[:, qt * 128:(qt + 1) * 128],
                                c.tri[:, :], ALU.add)
                        c.ev.inc(ins)
                    on(c, "vector", vtri)

                def se(s, hl=hl, qc=qc, ci=ci, diag=diag):
                    s.wait_ge(c.mm.h, c.mm.n)
                    if diag:
                        s.wait_ge(c.ev.h, c.ev.n)
                    ins = None
                    for qt in range(4):
                        w = (qt + 1) * 128 if diag else 512
                        ins = s.activation(
                            c.pb[:, qt * 512: qt * 512 + w],
                            c.PS[qt].ap()[:, 0:w], AF.Exp, scale=ISQD,
                            bias=c.biasH[:, qc * 4 + qt: qc * 4 + qt + 1],
                            accum_out=c.rsbH[:, (qc * 4 + qt) * 8 + ci:
                                             (qc * 4 + qt) * 8 + ci + 1])
                    c.sc.inc(ins)
                on(c, "scalar", se)

                tp_pairs = []
                for qt in range(4):
                    for tkt in range((qt + 1) if diag else 4):
                        tp_pairs.append(
                            (c.pb.rearrange("p (a n) -> p a n", n=128)[:, qt * 4 + tkt, :],
                             c.ptb.rearrange("p (a n) -> p a n", n=128)[:, tkt * 4 + qt, :]))

                def ttp(t, tp_pairs=tp_pairs):
                    t.wait_ge(c.sc.h, c.sc.n)
                    t.wait_ge(c.ev.h, c.ev.n)
                    ins = None
                    for i, (sp, dp) in enumerate(tp_pairs[:8]):
                        ins = t.transpose(c.PT[0].ap().rearrange("p (a n) -> p a n", n=128)[:, i, :], sp, c.ident[:, :])
                    c.mm.inc(ins)
                    if len(tp_pairs) > 8:
                        ins = None
                        for i, (sp, dp) in enumerate(tp_pairs[8:]):
                            ins = t.transpose(c.PT[1].ap().rearrange("p (a n) -> p a n", n=128)[:, i, :], sp, c.ident[:, :])
                        c.mm.inc(ins)
                on(c, "tensor", ttp)

                def vtp(v, tp_pairs=tp_pairs):
                    v.wait_ge(c.mm.h, c.mm.n)
                    ins = None
                    for i, (sp, dp) in enumerate(tp_pairs):
                        bank = c.PT[0] if i < 8 else c.PT[1]
                        ins = v.tensor_copy(
                            dp, bank.ap().rearrange("p (a n) -> p a n", n=128)[:, i % 8, :])
                    c.ev.inc(ins)
                on(c, "vector", vtp)

                def tpv(t, hl=hl, qc=qc, ci=ci, diag=diag):
                    t.wait_ge(c.ev.h, c.ev.n)
                    ins = None
                    for tkt in range(4):
                        off = tkt * 128 if diag else 0
                        ins = t.matmul(
                            c.PS[4].ap()[:, off:512],
                            VT[:, (ci * 4 + tkt) * 128:(ci * 4 + tkt + 1) * 128],
                            c.ptb[:, tkt * 512 + off: (tkt + 1) * 512],
                            start=(ci == 0 and tkt == 0),
                            stop=(diag and tkt == 3))
                    c.mm.inc(ins)
                on(c, "tensor", tpv)

            def vpv(v, hl=hl, qc=qc):
                v.wait_ge(c.mm.h, c.mm.n)
                ins = v.tensor_copy(
                    c.AR[4].ap().bitcast(F32)[:, qc * 512:(qc + 1) * 512],
                    c.PS[4].ap()[:, :])
                c.ev.inc(ins)
            on(c, "vector", vpv)

        def vrsH(v, hl=hl):
            v.wait_ge(c.sc.h, c.sc.n)
            v.tensor_reduce(c.rsumH,
                            c.rsbH.rearrange("p (a f) -> p a f", a=32),
                            AX.X, ALU.add)
            v.drain()
            ins = v.reciprocal(c.rsumH, c.rsumH)
            c.ev.inc(ins)
        on(c, "vector", vrsH)

        def trtH(t, hl=hl):
            t.wait_ge(c.ev.h, c.ev.n)
            ins = t.transpose(c.PS[5].ap()[0:32, 0:128], c.rsumH, c.identf[:, :])
            c.mm.inc(ins)
        on(c, "tensor", trtH)

        def vrtH(v, hl=hl):
            v.wait_ge(c.mm.h, c.mm.n)
            ins = v.tensor_copy(c.recipH, c.PS[5].ap()[0:32, 0:128])
            c.ev.inc(ins)
        on(c, "vector", vrtH)

        def grbH(g, hl=hl):
            g.wait_ge(c.ev.h, c.ev.n)
            ins = g.dma_start(c.rb, c.recipH)
            c.dS.inc(ins, 16)
            g.wait_ge(c.dS.h, c.dS.n)
        on(c, "gpsimd", grbH)

        rrm, bcm, fvm = {}, {}, {}
        for qc in range(8):
            sem = c.dsS[qc % 2]

            def grr(g, hl=hl, qc=qc, sem=sem):
                if qc >= 2:
                    g.wait_ge(c.mm.h, bcm[qc - 2])
                ins = g.dma_start(
                    c.rrow[0:1, (qc % 2) * 512:(qc % 2) * 512 + 512],
                    c.rb.rearrange("a n -> (a n)").unsqueeze(0)
                    [0:1, qc * 512:(qc + 1) * 512])
                sem.inc(ins, 16)
            on(c, "gpsimd", grr)
            rrm[qc] = sem.n

            def tbcH(t, hl=hl, qc=qc, sem=sem):
                t.wait_ge(sem.h, rrm[qc])
                if qc >= 1:
                    t.wait_ge(c.ev.h, fvm[qc - 1])
                ins = t.matmul(c.PS[5].ap()[:, :], c.ones_row[0:1, 0:128],
                               c.rrow[0:1, (qc % 2) * 512:(qc % 2) * 512 + 512],
                               start=True, stop=True)
                c.mm.inc(ins)
            on(c, "tensor", tbcH)
            bcm[qc] = c.mm.n

            def vfinH(v, hl=hl, qc=qc):
                v.wait_ge(c.mm.h, bcm[qc])
                ins = v.tensor_tensor(
                    att[:, hl * 4096 + qc * 512: hl * 4096 + (qc + 1) * 512],
                    c.AR[4].ap().bitcast(F32)[:, qc * 512:(qc + 1) * 512],
                    c.PS[5].ap()[:, :], ALU.mult)
                c.ev.inc(ins)
            on(c, "vector", vfinH)
            fvm[qc] = c.ev.n

    def g7(g):
        g.wait_ge(c.ev.h, c.ev.n)
        ins = None
        for hl in range(4):
            ins = g.dma_start(c.ag2_in[hl * 128:(hl + 1) * 128, :],
                              att[:, hl * 4096:(hl + 1) * 4096])
            c.dS.inc(ins, 16)
        g.wait_ge(c.dS.h, c.dS.n)
        ins = g.collective_compute(
            "AllGather", ALU.bypass,
            replica_groups=[[0, 1, 2, 3], [4, 5, 6, 7]],
            ins=[c.ag2_in], outs=[c.ag2_out])
        c.cc.inc(ins)
        g.wait_ge(c.cc.h, c.cc.n)
        ag2v = c.ag2_out_h.ap().rearrange("r (c n) -> (r c) n", c=8)
        ins = None
        for et in range(NDT):
            for ch in range(2):
                col = 96 + et * 2 + ch
                ins = g.indirect_dma_start(
                    VB.ap().rearrange("p (k t) -> p k t", k=NDT)[:, et, ch * 512:(ch + 1) * 512],
                    None, ag2v,
                    IndirectOffsetOnAxis(c.gidx_sb[:, col:col + 1], 0))
                c.dS.inc(ins, 16)
        g.wait_ge(c.dS.h, c.dS.n)
        ins = g.nop()
        c.gp.inc(ins)
    on(c, "gpsimd", g7)

    def tw7(t):
        t.wait_ge(c.gp.h, c.gp.n)
    on(c, "tensor", tw7)

    out_proj(c, VB, c.w_out, AR[0], QB, 1.0, c.b_out, wbuf2=XT.ap()[:, 0:8192])
    ln_epilogue(c, QB, AR[0], LN_EPS, junk=c.junkC)


def _dequant(c):
    nc = c.nc
    ar1i = c.AR[1].ap().bitcast(I32)
    nib = ar1i[:, 0:2048]
    bits = ar1i[:, 2048:4096]
    c1b = ar1i[:, 4096:6144]
    sgn = ar1i[:, 6144:8192]
    bitsf = c.AR[1].ap().bitcast(F32)[:, 2048:4096]
    wtile = c.AR[2].ap()[:, 0:2048]
    stg = c.AR[2].ap()[:, 2048:4096]

    for u in range(17):
        pk_src = (c.gup_sl[u * 128:(u + 1) * 128, :] if u < 11
                  else c.dnp_sl[(u - 11) * 128:(u - 10) * 128, :])
        sc_src = (c.gus_sl[u * 128:(u + 1) * 128, :] if u < 11
                  else c.dns_sl[(u - 11) * 128:(u - 10) * 128, :])

        def g(g, u=u, pk_src=pk_src, sc_src=sc_src):
            g.wait_ge(c.ev.h, c.ev.n)
            ins = g.dma_start(c.pk_sb[:, 0:256], pk_src)
            c.dS.inc(ins, 16)
            ins = g.dma_start(c.sc_sb[:, 0:16], sc_src)
            c.dS.inc(ins, 16)
        on(c, "gpsimd", g)

        def v(v, u=u):
            v.wait_ge(c.dS.h, c.dS.n)
            v.wait_ge(c.mm.h, c.mm.n)
            for j in range(8):
                v.tensor_scalar(nib[:, j:2048:8], c.pk_sb[:, 0:256], 4 * j, 15,
                                ALU.logical_shift_right, ALU.bitwise_and)
            v.drain()
            v.tensor_scalar(bits, nib, 7, 22, ALU.bitwise_and, ALU.logical_shift_left)
            v.drain()
            v.tensor_scalar(bits, bits, 0x3F000000, None, ALU.add)
            v.drain()
            v.tensor_scalar(c1b, bits, 0x3F400000, 0x400000, ALU.is_equal, ALU.mult)
            v.tensor_scalar(sgn, bits, 0x3F000000, 0x3F000000, ALU.is_equal, ALU.mult)
            v.drain()
            v.tensor_sub(bits, bits, c1b)
            v.drain()
            v.tensor_sub(bits, bits, sgn)
            v.tensor_scalar(sgn, nib, 8, 28, ALU.bitwise_and, ALU.logical_shift_left)
            v.drain()
            v.tensor_tensor(bits, bits, sgn, ALU.bitwise_or)
            v.drain()
            ins = v.tensor_tensor(
                wtile.rearrange("p (g k) -> p g k", k=128),
                bitsf.rearrange("p (g k) -> p g k", k=128),
                c.sc_sb[:, 0:16].unsqueeze(2).broadcast_to([128, 16, 128]),
                ALU.mult)
            c.ev.inc(ins)
        on(c, "vector", v)

        for half in range(2):
            lo, hi = half * 8, half * 8 + 8
            bank = c.PT[half % 2]

            def t(t, lo=lo, hi=hi, bank=bank):
                t.wait_ge(c.ev.h, c.ev.n)
                ins = None
                for i in range(lo, hi):
                    ins = t.transpose(bank.ap()[:, (i - lo) * 128:(i - lo + 1) * 128],
                                      wtile[:, i * 128:(i + 1) * 128], c.ident[:, :])
                c.mm.inc(ins)
            on(c, "tensor", t)

            def v2(v, lo=lo, hi=hi, bank=bank):
                v.wait_ge(c.mm.h, c.mm.n)
                v.wait_ge(c.dS.h, c.dS.n)
                ins = None
                for i in range(lo, hi):
                    ins = v.tensor_copy(stg[:, i * 128:(i + 1) * 128],
                                        bank.ap()[:, (i - lo) * 128:(i - lo + 1) * 128])
                c.ev.inc(ins)
            on(c, "vector", v2)

        def g2(g, u=u):
            g.wait_ge(c.ev.h, c.ev.n)
            ins = None
            for dt in range(16):
                ins = g.dma_start(
                    c.dq_in[u * D + dt * 128: u * D + (dt + 1) * 128, :],
                    stg[:, dt * 128:(dt + 1) * 128])
                c.dS.inc(ins, 16)
        on(c, "gpsimd", g2)

    def gag(g):
        g.wait_ge(c.dS.h, c.dS.n)
        ins = g.collective_compute(
            "AllGather", ALU.bypass, replica_groups=[list(range(8))],
            ins=[c.dq_in], outs=[c.dq_out])
        c.cc.inc(ins)
    on(c, "gpsimd", gag)
    c.deq_cc_mark = c.cc.n


def _dequant_finish(c):
    def gagf(g):
        g.wait_ge(c.cc.h, c.deq_cc_mark)
        ins = None
        for ug in range(48):
            ot, blk = ug // 3, ug % 3
            r, sl = ug // 6, ug % 6
            w = 1536 if blk == 2 else 2048
            ins = g.dma_start(
                c.wdn_t[blk * 2048: blk * 2048 + w, ot * 128:(ot + 1) * 128],
                c.dq_out[(r * 17 + 11 + sl) * D: (r * 17 + 11 + sl) * D + w, :])
            c.dS.inc(ins, 16)
        g.wait_ge(c.dS.h, c.dS.n)
        ins = g.nop()
        c.gp.inc(ins)
    on(c, "gpsimd", gagf)


def _gu_block(c, ot):
    r, sl = ot // 11, ot % 11
    base = (r * 17 + sl) * D
    return c.dq_out[base: base + D, :]


def _phase_d(c):
    nc = c.nc
    AR = c.AR
    XT, HB1, HB2, YB = AR[1], AR[2], AR[3], AR[4]
    xt = XT.ap().rearrange("p (k t) -> p k t", k=NDT)
    xa = AR[0].ap().rearrange("p (a d) -> p a d", a=NTT)
    ya = YB.ap().rearrange("p (a d) -> p a d", a=NTT)
    wb5 = c.AR[5].ap()

    _tap(c, "wdn", c.wdn_t, rows=IFF)
    def td0(t):
        t.wait_ge(c.ev.h, c.ev.n)
        t.wait_ge(c.sc.h, c.sc.n)
        t.wait_ge(c.dS.h, c.dS.n)
    on(c, "tensor", td0)
    transpose_stream(c, AR[0], XT)

    def hb_slice(it, a, b):
        if it < 32:
            return HB1.ap()[:, it * 512 + a: it * 512 + b]
        return HB2.ap()[:, (it - 32) * 512 + a: (it - 32) * 512 + b]

    NGRP = ((0, 15), (15, 15), (30, 14))

    for tc in range(2):
        load_mark, fillA, fillB, scm, evm = {}, {}, {}, {}, {}
        mm_base, ev_base, sc_base = c.mm.n, c.ev.n, c.sc.n

        def emit_load(ot, tc=tc):
            sem = c.dwS[ot % 2]

            def g(g, ot=ot, sem=sem):
                g.wait_ge(c.mm.h, fillB.get(ot - 2, mm_base))
                sl = (ot % 2) * 4096
                ins = g.dma_start(
                    wb5[:, sl:sl + 2048].rearrange("p (k e) -> p k e", e=128),
                    _gu_block(c, ot).rearrange("(k p) e -> p k e", p=128))
                sem.inc(ins, 16)
                ins = g.dma_start(
                    wb5[:, sl + 2048:sl + 4096].rearrange("p (k e) -> p k e", e=128),
                    _gu_block(c, NIT + ot).rearrange("(k p) e -> p k e", p=128))
                sem.inc(ins, 16)
            on(c, "gpsimd", g)
            load_mark[ot] = c.dwS[ot % 2].n

        emit_load(0)
        emit_load(1)
        for ot in range(NIT):
            bp = (ot % 2) * 2
            sl = (ot % 2) * 4096

            def t(t, tc=tc, ot=ot, bp=bp, sl=sl):
                t.wait_ge(c.dwS[ot % 2].h, load_mark[ot])
                t.wait_ge(c.sc.h, scm.get(ot - 2, sc_base))
                t.wait_ge(c.ev.h, evm.get(ot - 2, ev_base))
                ins = None
                for kt in range(NDT):
                    ins = t.matmul(c.PS[bp][:, :], wb5[:, sl + kt * 128:sl + (kt + 1) * 128],
                                   xt[:, kt, tc * 512:(tc + 1) * 512],
                                   start=(kt == 0), stop=(kt == NDT - 1))
                c.mm.inc(ins)
                fillA[ot] = c.mm.n
                ins = None
                for kt in range(NDT):
                    ins = t.matmul(c.PS[bp + 1][:, :],
                                   wb5[:, sl + 2048 + kt * 128:sl + 2048 + (kt + 1) * 128],
                                   xt[:, kt, tc * 512:(tc + 1) * 512],
                                   start=(kt == 0), stop=(kt == NDT - 1))
                c.mm.inc(ins)
            on(c, "tensor", t)
            fillB[ot] = c.mm.n

            def sil(s, ot=ot, bp=bp):
                s.wait_ge(c.mm.h, fillA[ot])
                ins = s.activation(c.sgB, c.PS[bp][:, :], AF.Silu)
                c.sc.inc(ins)
            on(c, "scalar", sil)
            scm[ot] = c.sc.n

            def v(v, ot=ot, bp=bp):
                v.wait_ge(c.sc.h, scm[ot])
                v.wait_ge(c.mm.h, fillB[ot])
                ins = v.tensor_tensor(hb_slice(ot, 0, 512), c.sgB,
                                      c.PS[bp + 1][:, :], ALU.mult)
                c.ev.inc(ins)
            on(c, "vector", v)
            evm[ot] = c.ev.n
            if ot + 2 < NIT:
                emit_load(ot + 2)

        if DEV_D_STOP == "gateup":
            continue
        GRP = [(0, 8), (8, 8), (16, 8), (24, 8), (32, 8), (40, 4)]
        for dc in range(4):
            load_mark2, fill2 = {}, {}
            mm_base2, ev_base2, sc_base2 = c.mm.n, c.ev.n, c.sc.n

            def emit_load2(qg, dc=dc):
                sem = c.dwS[qg % 2]

                def g(g, qg=qg, dc=dc, sem=sem):
                    g.wait_ge(c.mm.h, fill2.get(qg - 2, mm_base2))
                    i0, ng = GRP[qg]
                    sl = (qg % 2) * 4096
                    ins = g.dma_start(
                        wb5[:, sl:sl + ng * 512].rearrange("p (k e) -> p k e", e=512),
                        c.wdn_t[i0 * 128:(i0 + ng) * 128,
                                dc * 512:(dc + 1) * 512].rearrange("(k p) e -> p k e", p=128))
                    sem.inc(ins, 16)
                on(c, "gpsimd", g)
                load_mark2[qg] = c.dwS[qg % 2].n

            emit_load2(0)
            emit_load2(1)
            for qg in range(6):
                def t(t, tc=tc, dc=dc, qg=qg):
                    t.wait_ge(c.dwS[qg % 2].h, load_mark2[qg])
                    if qg == 0:
                        t.wait_ge(c.ev.h, ev_base2)
                        t.wait_ge(c.sc.h, sc_base2)
                    i0, ng = GRP[qg]
                    sl = (qg % 2) * 4096
                    ins = None
                    for hl in range(ng):
                        it = i0 + hl
                        for tt in range(4):
                            ins = t.matmul(
                                c.PS[tt][:, :],
                                hb_slice(it, tt * 128, (tt + 1) * 128),
                                wb5[:, sl + hl * 512:sl + (hl + 1) * 512],
                                start=(it == 0), stop=(it == NIT - 1))
                    c.mm.inc(ins)
                on(c, "tensor", t)
                fill2[qg] = c.mm.n
                if qg + 2 < 6:
                    emit_load2(qg + 2)

            def se(s, tc=tc, dc=dc):
                s.wait_ge(c.mm.h, c.mm.n)
                ins = None
                for tt in range(4):
                    ins = s.activation(c.junkD[:, 0:512], c.PS[tt][:, :], AF.Square,
                                       accum_out=c.stat2[:, 16 + tt * 4 + dc:17 + tt * 4 + dc])
                c.sc.inc(ins)
            on(c, "scalar", se)

            def ve(v, tc=tc, dc=dc):
                v.wait_ge(c.mm.h, c.mm.n)
                ins = None
                for tt in range(4):
                    tg = tc * 4 + tt
                    ins = v.tensor_copy(ya[:, tg, dc * 512:(dc + 1) * 512],
                                        c.PS[tt][:, :])
                c.ev.inc(ins)
            on(c, "vector", ve)

        if DEV_D_STOP == "down":
            continue
        def vr(v, tc=tc):
            v.wait_ge(c.sc.h, c.sc.n)
            v.wait_ge(c.ev.h, c.ev.n)
            v.tensor_reduce(c.stat2[:, 32:36],
                            c.stat2.ap().rearrange("p (a f) -> p a f", a=16)[:, 4:8, 0:4],
                            AX.X, ALU.add)
            v.drain()
            ins = v.tensor_scalar(c.stat2[:, 36:40], c.stat2[:, 32:36], 1.0 / D,
                                  RMS_EPS, ALU.mult, ALU.add)
            c.ev.inc(ins)
        on(c, "vector", vr)

        def sr(s, tc=tc):
            s.wait_ge(c.ev.h, c.ev.n)
            s.activation(c.stat2[:, 36:40], c.stat2[:, 36:40], AF.Ln)
            s.drain()
            ins = s.activation(c.stat2[:, 36:40], c.stat2[:, 36:40], AF.Exp, scale=-0.5)
            c.sc.inc(ins)
        on(c, "scalar", sr)

        def vv4(v, tc=tc):
            v.wait_ge(c.sc.h, c.sc.n)
            ins = None
            for tt in range(4):
                tg = tc * 4 + tt
                for q in range(4):
                    ins = v.scalar_tensor_tensor(
                        ya[:, tg, q * 512:(q + 1) * 512],
                        ya[:, tg, q * 512:(q + 1) * 512],
                        c.stat2[:, 36 + tt:37 + tt],
                        xa[:, tg, q * 512:(q + 1) * 512],
                        ALU.mult, ALU.add,
                        accum_out=c.stat[:, tg * 4 + q:tg * 4 + q + 1])
            c.ev.inc(ins)
        on(c, "vector", vv4)

    if DEV_D_STOP in ("gateup", "down"):
        def vfake(v):
            v.wait_ge(c.ev.h, c.ev.n)
            v.wait_ge(c.sc.h, c.sc.n)
            ins = v.tensor_copy(c.stgD, c.AR[0].ap().bitcast(F32)[:, 0:2048])
            c.ev.inc(ins)
        on(c, "vector", vfake)
        for tt in range(NTT):
            def syf(sy, tt=tt):
                sy.wait_ge(c.ev.h, c.ev.n)
                ins = sy.dma_start(c.y_out[tt * 128:(tt + 1) * 128, :], c.stgD)
                c.dO.inc(ins, 16)
            on(c, "sync", syf)
        return
    ln_epilogue_f32(c, YB, LN_EPS, junk=c.junkD)
    _tap(c, "statF", c.stat[:, :])
    _tap(c, "stat2F", c.stat2[:, :])



def _chunk_of(core):
    return core // 4, core % 4, 7 - core % 4


def _hadamard(n):
    i = np.arange(n)
    bits = (i[:, None] & i[None, :])
    pop = np.zeros((n, n), np.int64)
    x = bits.copy()
    while x.any():
        pop += x & 1
        x >>= 1
    return np.where(pop % 2 == 0, 1.0, -1.0).astype(np.float32)


def _host_consts(core):
    tri = np.triu(np.full((128, 128), -1e30, np.float32), k=1)
    cst = np.concatenate([tri, _hadamard(128)], axis=1).astype(np.float32)
    h64 = _hadamard(64)
    b, j0, j1 = _chunk_of(core)
    sel = np.zeros((16, 2), np.float32)
    for col, jj in enumerate((j0, j1)):
        for r in range(8):
            if r // 4 != b:
                continue
            for tc, jr in enumerate((r % 4, 7 - r % 4)):
                if jr < jj:
                    sel[2 * r + tc, col] = 1.0
    return cst, h64, sel


def _gidx_host(core):
    g, gr = core // 4, core % 4
    dh = np.arange(128)
    cols = []
    for hl in range(4):
        h = gr * 4 + hl
        for base in (0, 2048, 4096):
            for qc in range(8):
                i = qc if qc < 4 else 7 - qc
                sl = 0 if qc < 4 else 1
                cols.append((i * RBLK + base + h * 128 + dh) * 2 + sl)
    b, j0, j1 = _chunk_of(core)
    for et in range(NDT):
        for qc in (j0, j1):
            cols.append((et * 128 + dh) * 8 + qc)
    arr = np.zeros((128, 128), np.int32)
    a = np.stack(cols, 1).astype(np.int32)
    arr[:, :a.shape[1]] = a
    return arr


def _regt_host(core):
    g, gr = core // 4, core % 4
    r = np.zeros((1, 64), np.int32)
    for hl in range(4):
        h = gr * 4 + hl
        r[0, hl] = (6144 + h) * TLOC
        r[0, 36 + hl] = (6160 + h) * TLOC
    return r


def make_in_maps(inputs, phases=PHASES, x_override=None):
    x = x_override
    if x is None:
        x = np.ascontiguousarray(np.asarray(inputs["L0_attn_x"], np.float32).reshape(B * S, D))
    maps = []
    shared = {}
    if "A" in phases:
        shared["wq_t"] = np.ascontiguousarray(np.asarray(inputs["L0_attn_Wq"]).T)
        shared["wk_t"] = np.ascontiguousarray(np.asarray(inputs["L0_attn_Wk"]).T)
        shared["wv_t"] = np.ascontiguousarray(np.asarray(inputs["L0_attn_Wv"]).T)
        shared["wo_t"] = np.ascontiguousarray(np.asarray(inputs["L0_attn_Wo"]).T)
    if "B" in phases:
        shared["w1_t"] = np.ascontiguousarray(np.asarray(inputs["L0_ffn_W1"]).T)
        shared["w3_t"] = np.ascontiguousarray(np.asarray(inputs["L0_ffn_W3"]).T)
        shared["w2_t"] = np.ascontiguousarray(np.asarray(inputs["L0_ffn_W2"]).T)
    if "C" in phases:
        shared["w_in"] = np.ascontiguousarray(np.asarray(inputs["L1_attn_W_in"]))
        shared["b_in"] = np.ascontiguousarray(np.asarray(inputs["L1_attn_b_in"]).reshape(1, -1))
        shared["w_out"] = np.ascontiguousarray(np.asarray(inputs["L1_attn_W_out"]))
        shared["b_out"] = np.ascontiguousarray(np.asarray(inputs["L1_attn_b_out"]).reshape(1, -1))
    gu_packed = np.asarray(inputs["L1_ffn_gate_up_packed"]) if "D" in phases else None
    gu_scales = np.asarray(inputs["L1_ffn_gate_up_scales"]) if "D" in phases else None
    dn_packed = np.asarray(inputs["L1_ffn_down_packed"]) if "D" in phases else None
    dn_scales = np.asarray(inputs["L1_ffn_down_scales"]) if "D" in phases else None

    for core in range(8):
        b, j0, j1 = _chunk_of(core)
        rows = np.concatenate([
            x[b * S + j0 * CH: b * S + (j0 + 1) * CH],
            x[b * S + j1 * CH: b * S + (j1 + 1) * CH]], axis=0)
        cst, h64, sel = _host_consts(core)
        m = dict(shared)
        m.update({"x_in": np.ascontiguousarray(rows), "cst_f": cst, "cst2_f": h64})
        if "A" in phases:
            m["carry_sel"] = sel
        if "C" in phases:
            m["gidx"] = _gidx_host(core)
            m["regt"] = _regt_host(core)
        if "D" in phases:
            m["gup_sl"] = np.ascontiguousarray(gu_packed[core * 11 * 128:(core + 1) * 11 * 128])
            m["gus_sl"] = np.ascontiguousarray(gu_scales.T[core * 11 * 128:(core + 1) * 11 * 128])
            dnp_sl = np.zeros((6 * 128, 256), np.int32)
            dns_sl = np.zeros((6 * 128, 16), np.float16)
            for j in range(6):
                ug = 6 * core + j
                ot, blk = ug // 3, ug % 3
                wpc = 192 if blk == 2 else 256
                dnp_sl[j * 128:(j + 1) * 128, :wpc] = dn_packed[ot * 128:(ot + 1) * 128,
                                                               blk * 256: blk * 256 + wpc]
                dns_sl[j * 128:(j + 1) * 128, :wpc // 16] = \
                    dn_scales.T[ot * 128:(ot + 1) * 128, blk * 16: blk * 16 + wpc // 16]
            m["dnp_sl"] = dnp_sl
            m["dns_sl"] = dns_sl
        maps.append(m)
    return maps


def gather_out(res):
    out = np.zeros((B * S, D), np.float32)
    for core in range(8):
        b, j0, j1 = _chunk_of(core)
        r = np.asarray(res.results[core]["y_out"], np.float32)
        out[b * S + j0 * CH: b * S + (j0 + 1) * CH] = r[:CH]
        out[b * S + j1 * CH: b * S + (j1 + 1) * CH] = r[CH:]
    return out.reshape(B, S, D)


def kernel(**inputs):
    nc = build_program(PHASES)
    in_maps = make_in_maps(inputs, PHASES)
    res = run_bass_kernel_spmd(nc, in_maps, core_ids=list(range(8)))
    return gather_out(res)
